# version 54
# baseline (speedup 1.0000x reference)
import math
import numpy as np
import ml_dtypes
import concourse.bass as bass
import concourse.mybir as mybir
from concourse.bass_utils import run_bass_kernel_spmd

F32 = mybir.dt.float32
BF16 = mybir.dt.bfloat16
AF = mybir.ActivationFunctionType
ALU = mybir.AluOpType
AX = mybir.AxisListType

S = 2048
D = 1024
NH = 16
DH = 64
NP = NH // 2
NT = S // 128
NC = D // 128
NB = 4
WS = 512
RK = 256
DF = 4096
NF = DF // 128
EPS = 1e-5

SAME_ENGINE_SYNC = True


def build_program(debug=False):
    nc = bass.Bass("TRN2", target_bir_lowering=False)

    dx = nc.dram_tensor("x", [S, D], F32, kind="ExternalInput").ap()
    dWq = nc.dram_tensor("Wq", [NH, D, DH], F32, kind="ExternalInput").ap()
    dWk = nc.dram_tensor("Wk", [NH, D, DH], F32, kind="ExternalInput").ap()
    dWv = nc.dram_tensor("Wv", [NH, D, DH], F32, kind="ExternalInput").ap()
    dbq = nc.dram_tensor("bq", [NH, DH], F32, kind="ExternalInput").ap()
    dbk = nc.dram_tensor("bk", [NH, DH], F32, kind="ExternalInput").ap()
    dbv = nc.dram_tensor("bv", [NH, DH], F32, kind="ExternalInput").ap()
    dWo = nc.dram_tensor("Wo", [D, D], F32, kind="ExternalInput").ap()
    dbo = nc.dram_tensor("bo", [D], F32, kind="ExternalInput").ap()
    dU1 = nc.dram_tensor("U1", [D, RK], F32, kind="ExternalInput").ap()
    dV1 = nc.dram_tensor("V1", [RK, DF], F32, kind="ExternalInput").ap()
    db1 = nc.dram_tensor("b1", [DF], F32, kind="ExternalInput").ap()
    dU2 = nc.dram_tensor("U2", [DF, RK], F32, kind="ExternalInput").ap()
    dV2 = nc.dram_tensor("V2", [RK, D], F32, kind="ExternalInput").ap()
    db2 = nc.dram_tensor("b2", [D], F32, kind="ExternalInput").ap()
    dy = nc.dram_tensor("y", [S, D], F32, kind="ExternalOutput").ap()
    dV1i = nc.dram_tensor("V1i", [NF, 2, 128, 128], BF16, kind="Internal").ap()
    dU2i = nc.dram_tensor("U2i", [NF, 128, RK], BF16, kind="Internal").ap()

    dbg = {}
    def dbg_out(name, shape):
        if debug:
            dbg[name] = nc.dram_tensor("dbg_" + name, shape, F32, kind="ExternalOutput").ap()

    bf = ml_dtypes.bfloat16
    tri = np.triu(np.ones((128, 128))).astype(bf)
    c_ident = nc.inline_tensor(np.eye(128).astype(bf), "c_ident").ap()
    c_tri2 = nc.inline_tensor(np.concatenate([tri, tri], axis=1), "c_tri2").ap()
    c_or_bf = nc.inline_tensor(np.ones((1, 128), dtype=bf), "c_or_bf").ap()
    c_or_f32 = nc.inline_tensor(np.ones((1, 128), dtype=np.float32), "c_or_f32").ap()
    c_oc_bf = nc.inline_tensor(np.ones((128, 1), dtype=bf), "c_oc_bf").ap()
    c_oc_f32 = nc.inline_tensor(np.ones((128, 1), dtype=np.float32), "c_oc_f32").ap()
    _sc = np.zeros((128, 2), dtype=np.float32)
    _sc[:, 0] = EPS
    _sc[:, 1] = -math.log(8.0)
    c_scal = nc.inline_tensor(_sc, "c_scal").ap()

    A = nc.alloc_sbuf_tensor
    xnt = A("xnt", [128, NC * S], BF16)
    xnt_addr = nc.lookup_mloc(xnt).addr
    wobf = nc.alloc_sbuf_tensor_at("wobf", [128, 8192], BF16, offset=xnt_addr + 8192 * 2)
    QT = A("QT", [128, NP * S], BF16)
    KT = A("KT", [128, NP * S], BF16)
    Vst = A("Vst", [128, NT * 1056 + 64], BF16)
    x2sb = A("x2sb", [128, 4 * D], F32)
    x2_addr = nc.lookup_mloc(x2sb).addr
    wv_all = nc.alloc_sbuf_tensor_at("wv_all", [128, 8192], BF16, offset=x2_addr)
    OT = A("OT", [128, NC * WS], BF16)
    pt = [A("pt0", [128, 1024], BF16), A("pt1", [128, 1024], BF16), None]
    xn1tb = A("xn1tb", [128, 2 * D], BF16)
    xn1t = [xn1tb[:, 0:D], xn1tb[:, D:2 * D]]
    wstage2 = nc.alloc_sbuf_tensor_at("wstage2", [128, 1024], F32, offset=nc.lookup_mloc(xn1tb).addr)

    fftmp = A("fftmp", [128, D], F32)
    ff_addr = nc.lookup_mloc(fftmp).addr
    wstage = nc.alloc_sbuf_tensor_at("wstage", [128, 1024], F32, offset=ff_addr)
    xld = [A("xld0", [128, D], F32), A("xld1", [128, D], F32)]

    zbcP = A("zbcP", [128, 512], F32)
    lz = nc.alloc_sbuf_tensor_at("lz", [1, 1024], F32, offset=nc.lookup_mloc(xnt).addr + 8192)
    wbf = [A("wbf0", [128, 1024], BF16), A("wbf1", [128, 1024], BF16)]
    wbf0_addr = nc.lookup_mloc(wbf[0]).addr
    wbf1_addr = nc.lookup_mloc(wbf[1]).addr
    PTb = nc.alloc_sbuf_tensor_at("PTb", [128, 1024], BF16, offset=wbf0_addr)
    QFT = nc.alloc_sbuf_tensor_at("QFT", [128, 1024], BF16, offset=wbf1_addr)
    Hbf = [A("Hbf0", [128, 512], BF16), A("Hbf1", [128, 512], BF16), None]
    sq = nc.alloc_sbuf_tensor_at("sq", [128, 1024], BF16, offset=nc.lookup_mloc(OT).addr)
    sqB = nc.alloc_sbuf_tensor_at("sqB", [128, 1024], BF16, offset=nc.lookup_mloc(OT).addr + 2048)
    pt[2] = nc.alloc_sbuf_tensor_at("pt2", [128, 1024], BF16, offset=nc.lookup_mloc(xnt).addr + 12288)
    wst2 = nc.alloc_sbuf_tensor_at("wst2", [128, 512], F32, offset=nc.lookup_mloc(OT).addr + 4096)
    trfb = nc.alloc_sbuf_tensor_at("trfb", [128, 512], BF16, offset=nc.lookup_mloc(OT).addr + 6144)
    Hbf[2] = nc.alloc_sbuf_tensor_at("Hbf2", [128, 512], BF16, offset=nc.lookup_mloc(OT).addr + 7168)
    wst2b = A("wst2b", [128, 512], F32)
    trfbb = A("trfbb", [128, 512], BF16)
    brstage = nc.alloc_sbuf_tensor_at("brstage", [1, 1024], F32, offset=ff_addr)
    U1bf = A("U1bf", [128, 2048], BF16)
    V2bf = A("V2bf", [128, 2048], BF16)
    V1t = [A("V1t%d" % q, [128, 256], BF16) for q in range(5)]

    U2t = [A("U2t%d" % q, [128, 256], BF16) for q in range(5)]

    stb = [A("st%d" % q, [128, 12], F32) for q in range(8)]
    agb = [A("ag%d" % q, [128, 2], F32) for q in range(8)]
    lnvb = [A("lnv%d" % q, [128, 1], F32) for q in range(8)]
    rsb = [A("rs%d" % q, [128, 1], F32) for q in range(8)]
    qs1 = A("qs1", [128, 2], F32)
    st32 = A("st32", [128, 32], F32)
    t2b = A("t2b", [128, 32], F32)
    n8b = A("n8b", [128, 32], F32)
    btqk = A("btqk", [1, 16], F32)
    ltq = A("ltq", [1, 16], F32)
    lsq = A("lsq", [1, 8], F32)
    Brow = A("Brow", [1, 8], F32)
    nBrow = A("nBrow", [1, 8], F32)
    negBb = A("negBb", [128, 8], F32)
    bqs = A("bqs", [128, 8], F32)
    bks = A("bks", [128, 8], F32)
    b1s = A("b1s", [128, 32], F32)
    borow = A("borow", [1, 1024], BF16)
    b2row = A("b2row", [1, 1024], BF16)
    bvrow = A("bvrow", [1, 1024], BF16)

    ident = A("ident", [128, 128], BF16)
    tri2 = A("tri2", [128, 256], BF16)
    or_bf = A("or_bf", [1, 128], BF16)
    or_f32 = A("or_f32", [1, 128], F32)
    oc_bf = A("oc_bf", [128, 1], BF16)
    oc_f32 = A("oc_f32", [128, 1], F32)
    scal = A("scal", [128, 2], F32)

    P = nc.alloc_psum_tensor
    pS = P("pS", [128, 1024], F32)
    pO = P("pO", [128, 1024], F32)
    pG12 = P("pG12", [128, 1024], F32)
    pG1 = pG12[:, 0:512]
    pG2 = pG12[:, 512:1024]
    pG3 = P("pG3", [128, 512], F32)
    pTP = P("pTP", [128, 1024], BF16)

    OPS = []

    def op(engine, fn, reads=(), writes=(), dma=False, slot=None):
        OPS.append({"e": engine, "f": fn, "r": tuple(reads), "w": tuple(writes),
                    "inc": 16 if dma else 1, "dma": dma,
                    "slot": (slot or (writes[0] if writes else "misc")) if dma else None})

    def warm():
        op("tensor", lambda e: e.ldweights(ident[0:1, 0:1]), ("ident",), ())

    op("gpsimd", lambda e: e.dma_start(ident[:], c_ident), (), ("ident",), dma=True)
    op("gpsimd", lambda e: e.dma_start(tri2[:], c_tri2), (), ("tri2",), dma=True)
    op("gpsimd", lambda e: e.dma_start(or_bf[:], c_or_bf), (), ("or_bf",), dma=True)
    op("gpsimd", lambda e: e.dma_start(or_f32[:], c_or_f32), (), ("or_f32",), dma=True)
    op("gpsimd", lambda e: e.dma_start(oc_bf[:], c_oc_bf), (), ("oc_bf",), dma=True)
    op("gpsimd", lambda e: e.dma_start(oc_f32[:], c_oc_f32), (), ("oc_f32",), dma=True)
    op("gpsimd", lambda e: e.dma_start(scal[:], c_scal), (), ("scal",), dma=True)
    for nm, src, dst in (("bqs", dbq, bqs), ("bks", dbk, bks)):
        op("gpsimd", lambda e, s=src, d=dst: e.dma_start(
            d[:], s.rearrange("(p two) d -> (two d) p", two=2),
            allow_slow_non_contiguous=True), (), (nm,), dma=True)
    op("gpsimd", lambda e: e.dma_start(
        b1s[:], db1.rearrange("(c P) -> P c", P=128),
        allow_slow_non_contiguous=True), (), ("b1s",), dma=True)
    for nm, src, dst in (("borow", dbo, borow), ("b2row", db2, b2row)):
        op("gpsimd", lambda e, s=src: e.dma_start(brstage[:], s.rearrange("(a d) -> a d", a=1)),
           ("WSTG",), ("brstage", "WSTG"), dma=True)
        op("vector", lambda e, d=dst: e.tensor_copy(d[:], brstage[:]), ("brstage",), (nm,))
    op("gpsimd", lambda e: e.dma_start(brstage[:], dbv.rearrange("(a h) d -> a (h d)", a=1)),
       ("WSTG",), ("brstage", "WSTG"), dma=True)
    op("vector", lambda e: e.tensor_copy(bvrow[:], brstage[:]), ("brstage",), ("bvrow",))
    op("vector", lambda e: e.memset(Vst[:], 0.0), (), ("VST",))
    op("vector", lambda e: e.memset(
        Vst[:, 0:NT * 1056].rearrange("p (t g c) -> p t g c", t=NT, g=16)[:, :, :, 64:65], 1.0),
        ("VST",), ("VST",))

    def emit_ln_stats(src_tiles, bi, extra_reads):
        st, ag, lnv, rs = stb[bi], agb[bi], lnvb[bi], rsb[bi]
        kst, kag, klnv, krs = "st%d" % bi, "ag%d" % bi, "lnv%d" % bi, "rs%d" % bi
        op("vector", lambda e: e.bn_stats(st[:, 0:6], src_tiles[0]), extra_reads, (kst,))
        op("vector", lambda e: e.bn_stats(st[:, 6:12], src_tiles[1]), extra_reads, (kst,))
        op("vector", lambda e: e.bn_aggr(ag[:], st[:]), (kst,), (kag,))
        op("scalar", lambda e: e.activation(lnv[:], ag[:, 1:2], AF.Ln, bias=scal[:, 0:1]),
           (kag, "scal"), (klnv,))
        op("scalar", lambda e: e.activation(rs[:], lnv[:], AF.Exp, scale=-0.5), (klnv,), (krs,))
        warm()
        return ag, rs, kag, krs

    def emit_ln1_tp(i):
        nb = i % 2
        kn = "xn1t%d" % nb
        for c in range(NC):
            op("tensor", lambda e, nb=nb, c=c: e.transpose(
                pTP[:, c * 128:(c + 1) * 128], xn1t[nb][:, c * 128:(c + 1) * 128], ident[:]),
                (kn, "ident"), ("pTP",))
        if i % 2 == 1:
            op("scalar", lambda e, i=i: e.activation(
                xnt[:].rearrange("p (c s) -> p c s", c=NC)[:, :, i * 128:(i + 1) * 128],
                pTP[:].rearrange("p (c s) -> p c s", c=NC), AF.Copy),
                ("pTP",), ("XNT", "xnti%d" % i))
        else:
            op("vector", lambda e, i=i: e.tensor_copy(
                xnt[:].rearrange("p (c s) -> p c s", c=NC)[:, :, i * 128:(i + 1) * 128],
                pTP[:].rearrange("p (c s) -> p c s", c=NC)),
                ("pTP",), ("XNT", "xnti%d" % i))

    for i in range(NT):
        xb = i % 2
        kx = "xld%d" % xb
        xeng = "gpsimd" if i % 2 == 0 else "scalar"
        op(xeng, lambda e, i=i, xb=xb: e.dma_start(xld[xb][:], dx[i * 128:(i + 1) * 128, :]),
           (), (kx,), dma=True)
        bi = i % 4
        ag, rs, kag, krs = emit_ln_stats(
            [xld[xb][:, 0:512], xld[xb][:, 512:1024]], bi, (kx,))
        nb = i % 2
        kn = "xn1t%d" % nb
        if i % 2 == 1:
            lnv, klnv = lnvb[bi], "lnv%d" % bi
            op("vector", lambda e, ag=ag, rs=rs, lnv=lnv: e.scalar_tensor_tensor(
                lnv[:], ag[:, 0:1], -1.0, rs[:], ALU.mult, ALU.mult),
                (kag, krs), (klnv,))
            op("scalar", lambda e, xb=xb, nb=nb, rs=rs, lnv=lnv: e.activation(
                xn1t[nb], xld[xb][:], AF.Identity, bias=lnv[:], scale=rs[:]),
                (kx, krs, klnv), (kn,))
        else:
            op("vector", lambda e, xb=xb, nb=nb, ag=ag, rs=rs: e.tensor_scalar(
                xn1t[nb], xld[xb][:], ag[:, 0:1], rs[:], ALU.subtract, ALU.mult),
                (kx, kag, krs), (kn,))
        if i >= 1:
            emit_ln1_tp(i - 1)
    emit_ln1_tp(NT - 1)

    stage_cnt = [0]

    def stage_into(emitters, dstfn, kd, extra_w=()):
        par = stage_cnt[0] % 2
        stage_cnt[0] += 1
        ws = wstage if par == 0 else wstage2
        kws = "WSTG.%d" % par
        ex = ("WSTG",) if par == 0 else ("xn1t0", "xn1t1")
        for fn in emitters:
            op("sync", lambda e, fn=fn, ws=ws: fn(e, ws), (kws,), (kws,) + ex, dma=True)
        op("vector", lambda e, ws=ws: dstfn(e, ws), (kws,), (kd,) + ex + tuple(extra_w))

    def stage_w_pair(dW, p, dstbuf):
        kd = "wbf%d" % dstbuf
        stage_into(
            [lambda e, ws, dW=dW, p=p: e.dma_start(
                ws[:].rearrange("p (c d) -> p c d", c=NC)[:, :, 0:64],
                dW[2 * p].rearrange("(c P) d -> P c d", P=128)),
             lambda e, ws, dW=dW, p=p: e.dma_start(
                ws[:].rearrange("p (c d) -> p c d", c=NC)[:, :, 64:128],
                dW[2 * p + 1].rearrange("(c P) d -> P c d", P=128))],
            lambda e, ws, b=dstbuf: e.tensor_copy(wbf[b][:], ws[:]), kd)
        return kd

    def emit_transform_loads(f):
        par = f % 2
        w2 = wst2 if par == 0 else wst2b
        kw2v = "WST2V.%d" % par
        kw2u = "WST2U.%d" % par
        ex = ("OTp4", "OTp5") if par == 0 else ()
        op("gpsimd", lambda e, f=f, w2=w2: e.dma_start(
            w2[:, 0:256].rearrange("p (rc c) -> p rc c", rc=2),
            dV1[:, f * 128:(f + 1) * 128].rearrange("(rc P) c -> P rc c", P=128)),
            (kw2v,), (kw2v,) + ex, dma=True, slot="W2V%d" % par)
        op("gpsimd", lambda e, f=f, w2=w2: e.dma_start(w2[:, 256:512], dU2[f * 128:(f + 1) * 128, :]),
           (kw2u,), (kw2u,) + ex, dma=True, slot="W2U%d" % par)

    def emit_transform_rest(f):
        par = f % 2
        w2, tb = (wst2, trfb) if par == 0 else (wst2b, trfbb)
        kw2v = "WST2V.%d" % par
        kw2u = "WST2U.%d" % par
        ktbv = "TRFBV.%d" % par
        ktbu = "TRFBU.%d" % par
        ex = ("OTp4", "OTp5", "OTp6") if par == 0 else ()
        op("gpsimd", lambda e, w2=w2, tb=tb: e.tensor_copy(tb[:, 0:256], w2[:, 0:256]),
           (kw2v,), (ktbv, kw2v) + ex)
        op("gpsimd", lambda e, f=f, tb=tb: e.dma_start(
            dV1i[f].rearrange("rc p c -> p rc c"),
            tb[:, 0:256].rearrange("p (rc c) -> p rc c", rc=2)),
            (ktbv,) + ex, ("V1I",), dma=True, slot="V1I")
        op("gpsimd", lambda e, w2=w2, tb=tb: e.tensor_copy(tb[:, 256:512], w2[:, 256:512]),
           (kw2u,), (ktbu, kw2u) + ex)
        op("gpsimd", lambda e, f=f, tb=tb: e.dma_start(dU2i[f], tb[:, 256:512]),
           (ktbu,) + ex, ("U2I",), dma=True, slot="U2I")

    for p in range(NP):
        for f in range(4 * p, 4 * p + 4):
            emit_transform_loads(f)
            if f >= 1:
                emit_transform_rest(f - 1)
        for proj, dW, dst, kdst, bias in (("q", dWq, QT, "QTm", bqs), ("k", dWk, KT, "KTm", bks)):
            wb = (2 * p + (0 if proj == "q" else 1)) % 2
            kw = stage_w_pair(dW, p, wb)
            kbias = "bqs" if proj == "q" else "bks"
            for b in range(NB):
                pg, kpg = (pG1, "pG1") if b % 2 == 0 else (pG2, "pG2")
                for c in range(NC):
                    op("tensor", lambda e, wb=wb, c=c, b=b, p=p, pg=pg: e.matmul(
                        pg[:], wbf[wb][:, c * 128:(c + 1) * 128],
                        xnt[:, c * S + b * WS: c * S + (b + 1) * WS],
                        start=(c == 0), stop=(c == NC - 1)),
                        (kw, "XNT"), (kpg,))
                op("scalar", lambda e, p=p, b=b, dst=dst, bias=bias, pg=pg: e.activation(
                    dst[:, p * S + b * WS: p * S + (b + 1) * WS], pg[:], AF.Identity,
                    bias=bias[:, p:p + 1]), (kpg, kbias), (kdst,))

    emit_transform_rest(NF - 1)
    for c in range(NC):
        for o in range(2):
            stage_into(
                [lambda e, ws, c=c, o=o: e.dma_start(
                    ws[:, 0:512].rearrange("p (h d) -> p h d", h=8),
                    dWv[8 * o:8 * o + 8, c * 128:(c + 1) * 128, :].rearrange("h P d -> P h d"))],
                lambda e, ws, c=c, o=o: e.tensor_copy(
                    wv_all[:, c * 1024 + o * 512: c * 1024 + (o + 1) * 512], ws[:, 0:512]), "WV")
    for i in range(NT):
        for o in range(2):
            pg, kpg = (pG2, "pG2") if (2 * i + o) % 2 == 0 else (pG3, "pG3")
            for c in range(NC):
                op("tensor", lambda e, i=i, o=o, c=c, pg=pg: e.matmul(
                    pg[:], xnt[:, c * S + i * 128: c * S + (i + 1) * 128],
                    wv_all[:, c * 1024 + o * 512: c * 1024 + (o + 1) * 512],
                    start=(c == 0), stop=False), ("XNT", "WV"), (kpg,))
            op("tensor", lambda e, o=o, pg=pg: e.matmul(
                pg[:], or_bf[:], bvrow[0:1, o * 512:(o + 1) * 512],
                start=False, stop=True), ("or_bf", "bvrow"), (kpg,))
            op("vector", lambda e, i=i, o=o, pg=pg: e.tensor_copy(
                Vst[:, i * 1056 + (4 * o) * 132: i * 1056 + (4 * o + 4) * 132]
                   .rearrange("p (pr h d) -> p pr h d", pr=4, h=2)[:, :, :, 0:64],
                pg[:].rearrange("p (pr h d) -> p pr h d", pr=4, h=2)),
                (kpg,), ("VST",))

    for p in range(NP):
        for ti, (src, kt) in enumerate(((QT, "QTm"), (KT, "KTm"))):
            for half in range(2):
                sqb, ksq = (sq, "sq") if (2 * ti + half) % 2 == 0 else (sqB, "sqB")
                kex = ("OTp0", "OTp1") if (2 * ti + half) % 2 == 0 else ("OTp2", "OTp3")
                op("vector", lambda e, src=src, p=p, half=half, sqb=sqb: e.tensor_mul(
                    sqb[:], src[:, p * S + half * 1024: p * S + (half + 1) * 1024],
                    src[:, p * S + half * 1024: p * S + (half + 1) * 1024]), (kt,), (ksq,) + kex)
                for j in range(8):
                    col = ti * 16 + half * 8 + j
                    op("tensor", lambda e, j=j, col=col, sqb=sqb: e.matmul(
                        pG3[:, col:col + 1], sqb[:, j * 128:(j + 1) * 128], oc_bf[:],
                        start=True, stop=True), (ksq, "oc_bf"), ("pG3",))
        op("vector", lambda e: e.tensor_copy(st32[:], pG3[:, 0:32]), ("pG3",), ("st32",))
        op("vector", lambda e: e.tensor_mul(t2b[:], st32[:], st32[:]), ("st32",), ("t2b",))
        op("vector", lambda e: e.tensor_mul(n8b[:], t2b[:], t2b[:]), ("t2b",), ("n8b",))
        op("vector", lambda e: e.tensor_reduce(qs1[:, 0:1], n8b[:, 0:16], AX.X, ALU.add),
           ("n8b",), ("qs1",))
        op("vector", lambda e: e.tensor_reduce(qs1[:, 1:2], n8b[:, 16:32], AX.X, ALU.add),
           ("n8b",), ("qs1",))
        op("tensor", lambda e, p=p: e.matmul(pG3[0:1, 40 + p:41 + p], qs1[:, 0:1], oc_f32[:],
                                             start=True, stop=True), ("qs1", "oc_f32"), ("pG3",))
        op("tensor", lambda e, p=p: e.matmul(pG3[0:1, 48 + p:49 + p], qs1[:, 1:2], oc_f32[:],
                                             start=True, stop=True), ("qs1", "oc_f32"), ("pG3",))
    op("vector", lambda e: e.tensor_copy(btqk[:], pG3[0:1, 40:56]), ("pG3",), ("btqk",))
    op("scalar", lambda e: e.activation(ltq[:], btqk[:], AF.Ln), ("btqk",), ("ltq",))
    op("vector", lambda e: e.tensor_add(lsq[:], ltq[0:1, 0:8], ltq[0:1, 8:16]), ("ltq",), ("lsq",))
    op("scalar", lambda e: e.activation(Brow[:], lsq[:], AF.Exp, scale=0.125,
                                        bias=scal[0:1, 1:2]), ("lsq", "scal"), ("Brow",))
    op("vector", lambda e: e.tensor_scalar_mul(nBrow[:], Brow[:], -1.0), ("Brow",), ("nBrow",))
    for p in range(NP):
        op("tensor", lambda e, p=p: e.matmul(pG3[:, 64 + p:65 + p], or_f32[:],
                                             nBrow[0:1, p:p + 1], start=True, stop=True),
           ("nBrow", "or_f32"), ("pG3",))
    op("vector", lambda e: e.tensor_copy(negBb[:], pG3[:, 64:72]), ("pG3",), ("negBb",))

    for c in range(NC):
        stage_into(
            [lambda e, ws, c=c: e.dma_start(ws[:], dWo[c * 128:(c + 1) * 128, :])],
            lambda e, ws, c=c: e.tensor_copy(wobf[:, c * 1024:(c + 1) * 1024], ws[:]),
            "wobf%d" % c, extra_w=("XNT",))
    for half in range(2):
        stage_into(
            [lambda e, ws, half=half: e.dma_start(
                ws[:].rearrange("p (c r) -> p c r", c=4),
                dU1[half * 512:(half + 1) * 512, :].rearrange("(c P) r -> P c r", P=128))],
            lambda e, ws, half=half: e.tensor_copy(
                U1bf[:, half * 1024:(half + 1) * 1024], ws[:]), "U1bf")
    for rc in range(2):
        stage_into(
            [lambda e, ws, rc=rc: e.dma_start(ws[:], dV2[rc * 128:(rc + 1) * 128, :])],
            lambda e, ws, rc=rc: e.tensor_copy(V2bf[:, rc * 1024:(rc + 1) * 1024], ws[:]), "V2bf")

    gchunk = [0]
    for k in range(NB):
        ntc = 4 * (k + 1)
        for pre_i in (0, 1):
            i0 = 4 * k + pre_i
            op("scalar", lambda e, i0=i0, xb=pre_i % 2: e.dma_start(
                xld[xb][:], dx[i0 * 128:(i0 + 1) * 128, :]), (), ("xld%d" % (pre_i % 2),), dma=True)

        def make_epilogue(p):
            def emit_epi1():
                op("vector", lambda e, p=p: e.tensor_copy(
                    OT[0:64, p * 512:(p + 1) * 512], pO[0:64, 0:512]),
                    ("pO",), ("OTm", "OTp%d" % p))
                op("vector", lambda e, p=p: e.tensor_copy(
                    OT[64:128, p * 512:(p + 1) * 512], pO[0:64, 512:1024]),
                    ("pO",), ("OTm", "OTp%d" % p))
                op("scalar", lambda e: e.activation(lz[:], pO[64:65, :], AF.Ln),
                   ("pO",), ("lz", "XNT"))
                warm()

            def emit_epi2():
                op("tensor", lambda e: e.matmul(pG3[0:64, :], or_f32[0:1, 0:64], lz[0:1, 0:512],
                                                start=True, stop=True), ("lz", "or_f32"), ("pG3",))
                op("tensor", lambda e: e.matmul(pG3[64:128, :], or_f32[0:1, 0:64], lz[0:1, 512:1024],
                                                start=True, stop=True), ("lz", "or_f32"), ("pG3",))
                op("scalar", lambda e: e.activation(zbcP[:], pG3[:], AF.Exp, scale=-1.0),
                   ("pG3",), ("zbcP",))
                op("vector", lambda e, p=p: e.tensor_mul(
                    OT[0:64, p * 512:(p + 1) * 512],
                    OT[0:64, p * 512:(p + 1) * 512], zbcP[0:64, :]),
                    ("zbcP", "OTm", "OTp%d" % p), ("OTm", "OTp%d" % p))
                op("vector", lambda e, p=p: e.tensor_mul(
                    OT[64:128, p * 512:(p + 1) * 512],
                    OT[64:128, p * 512:(p + 1) * 512], zbcP[64:128, :]),
                    ("zbcP", "OTm", "OTp%d" % p), ("OTm", "OTp%d" % p))
            return emit_epi1, emit_epi2

        def emit_pv(p, info, first, last):
            j, off, width, pb = info
            kpt = "pt%d" % pb
            for h in range(2):
                op("tensor", lambda e, p=p, j=j, h=h, off=off, width=width, pb=pb,
                   st=first, sp=last: e.matmul(
                    pO[:, h * 512 + off: h * 512 + off + width],
                    Vst[:, j * 1056 + p * 132 + h * 66: j * 1056 + p * 132 + h * 66 + 128],
                    pt[pb][:, h * 512 + off: h * 512 + off + width],
                    start=st, stop=sp), (kpt, "VST"), ("pO",))

        pending_epi = None
        for p in range(NP):
            pend_pv = []
            for j in range(ntc):
                r = j - 4 * k
                off = 128 * r if r >= 0 else 0
                width = WS - off
                pb = gchunk[0] % 3
                sparity = gchunk[0] % 2
                gchunk[0] += 1
                kpt = "pt%d" % pb
                kptw = (kpt, "XNT") if pb == 2 else (kpt,)
                if sparity == 0:
                    svs = [(pS[:, off:off + width], "pSa"),
                           (pS[:, 512 + off: 512 + off + width], "pSb")]
                else:
                    svs = [(pG1[:, off:off + width], "pG1"),
                           (pG2[:, off:off + width], "pG2")]
                for h in range(2):
                    op("tensor", lambda e, p=p, j=j, h=h, off=off, width=width, k=k,
                       sv=svs[h][0]: e.matmul(
                        sv,
                        KT[h * 64:(h + 1) * 64, p * S + j * 128: p * S + (j + 1) * 128],
                        QT[h * 64:(h + 1) * 64, p * S + k * WS + off: p * S + k * WS + off + width],
                        start=True, stop=True, tile_position=(64 * h, 0)),
                        ("QTm", "KTm"), (svs[h][1],))
                if j == 1 and pending_epi is not None:
                    pending_epi[0]()
                if j == 3 and pending_epi is not None:
                    pending_epi[1]()
                    pending_epi = None
                if len(pend_pv) >= 2:
                    info = pend_pv.pop(0)
                    emit_pv(p, info, first=(info[0] == 0), last=False)
                if sparity == 0:
                    op("scalar", lambda e, p=p, pb=pb, off=off, width=width: e.activation(
                        pt[pb][:].rearrange("p (h s) -> p h s", h=2)[:, :, off:off + width],
                        pS[:].rearrange("p (h s) -> p h s", h=2)[:, :, off:off + width],
                        AF.Exp, bias=negBb[:, p:p + 1], scale=0.125),
                        ("pSa", "pSb", "negBb"), kptw)
                else:
                    op("scalar", lambda e, p=p, pb=pb, off=off, width=width: e.activation(
                        pt[pb][:].rearrange("p (h s) -> p h s", h=2)[:, :, off:off + width],
                        pG12[:].rearrange("p (h s) -> p h s", h=2)[:, :, off:off + width],
                        AF.Exp, bias=negBb[:, p:p + 1], scale=0.125),
                        ("pG1", "pG2", "negBb"), kptw)
                if r >= 0:
                    op("vector", lambda e, pb=pb, off=off: e.tensor_mul(
                        pt[pb][:].rearrange("p (h s) -> p h s", h=2)[:, :, off:off + 128],
                        pt[pb][:].rearrange("p (h s) -> p h s", h=2)[:, :, off:off + 128],
                        tri2[:].rearrange("p (h s) -> p h s", h=2)),
                        (kpt, "tri2"), kptw)
                pend_pv.append((j, off, width, pb))
                warm()
            if pending_epi is not None:
                pending_epi()
                pending_epi = None
            while pend_pv:
                info = pend_pv.pop(0)
                emit_pv(p, info, first=(info[0] == 0), last=(len(pend_pv) == 0))
            pending_epi = make_epilogue(p)
        pending_epi[0]()
        pending_epi[1]()
        pending_epi = None
        def emit_ln3_stats(irel):
            bi3 = 4 + irel
            kx2 = "x2:%d" % irel
            ag3, rs3, kag3, krs3 = emit_ln_stats(
                [x2sb[:, irel * D: irel * D + 512], x2sb[:, irel * D + 512: (irel + 1) * D]],
                bi3, (kx2,))
            kn3 = "xn1t%d" % (irel % 2)
            op("vector", lambda e, irel=irel, nb=irel % 2, ag=ag3, rs=rs3: e.tensor_scalar(
                xn1t[nb], x2sb[:, irel * D:(irel + 1) * D], ag[:, 0:1], rs[:],
                ALU.subtract, ALU.mult), (kx2, kag3, krs3), (kn3,))

        def emit_ln3_tp(irel):
            kn3 = "xn1t%d" % (irel % 2)
            for c in range(NC):
                op("tensor", lambda e, nb=irel % 2, c=c: e.transpose(
                    pTP[:, c * 128:(c + 1) * 128], xn1t[nb][:, c * 128:(c + 1) * 128], ident[:]),
                    (kn3, "ident"), ("pTP",))
            op("vector", lambda e, irel=irel: e.tensor_copy(
                xnt[:, 0:4096].rearrange("p (c s) -> p c s", c=NC)[:, :, irel * 128:(irel + 1) * 128],
                pTP[:].rearrange("p (c s) -> p c s", c=NC)),
                ("pTP",), ("XNT",) + tuple("xnti%d" % q for q in range(16)))
            warm()

        for irel in range(4):
            i = 4 * k + irel
            xb = i % 2
            kx = "xld%d" % xb
            if irel % 2 == 0:
                ehbanks = ((0, pG1, "pG1"), (1, pG2, "pG2"))
            else:
                ehbanks = ((0, pS[:, 0:512], "pSa"), (1, pS[:, 512:1024], "pSb"))
            for eh, pg, kpg in ehbanks:
                for c in range(NC):
                    op("tensor", lambda e, c=c, irel=irel, eh=eh, pg=pg: e.matmul(
                        pg, OT[:, c * 512 + irel * 128: c * 512 + (irel + 1) * 128],
                        wobf[:, c * 1024 + eh * 512: c * 1024 + (eh + 1) * 512],
                        start=(c == 0), stop=False), ("OTm", "wobf%d" % c), (kpg,))
                op("tensor", lambda e, eh=eh, pg=pg: e.matmul(
                    pg, or_bf[:], borow[0:1, eh * 512:(eh + 1) * 512],
                    start=False, stop=True), ("or_bf", "borow"), (kpg,))
                warm()
            bi = i % 4
            st, ag, lnv, rs = stb[bi], agb[bi], lnvb[bi], rsb[bi]
            kst, kag, klnv, krs = "st%d" % bi, "ag%d" % bi, "lnv%d" % bi, "rs%d" % bi
            op("vector", lambda e, st=st, pg=ehbanks[0][1], kpg=ehbanks[0][2]: e.bn_stats(
                st[:, 0:6], pg), (ehbanks[0][2],), (kst,))
            op("vector", lambda e, st=st, pg=ehbanks[1][1]: e.bn_stats(
                st[:, 6:12], pg), (ehbanks[1][2],), (kst,))
            op("vector", lambda e, st=st, ag=ag: e.bn_aggr(ag[:], st[:]), (kst,), (kag,))
            op("scalar", lambda e, ag=ag, lnv=lnv: e.activation(lnv[:], ag[:, 1:2],
               AF.Ln, bias=scal[:, 0:1]), (kag, "scal"), (klnv,))
            op("scalar", lambda e, lnv=lnv, rs=rs: e.activation(rs[:], lnv[:], AF.Exp, scale=-0.5),
               (klnv,), (krs,))
            for eh, pg, kpg in ehbanks:
                op("vector", lambda e, pg=pg, ag=ag, rs=rs: e.tensor_scalar(
                    pg, pg, ag[:, 0:1], rs[:], ALU.subtract, ALU.mult),
                    (kpg, kag, krs), (kpg,))
                op("vector", lambda e, pg=pg, irel=irel, eh=eh, xb=xb: e.tensor_add(
                    x2sb[:, irel * D + eh * 512: irel * D + (eh + 1) * 512],
                    xld[xb][:, eh * 512:(eh + 1) * 512], pg),
                    (kpg, kx), ("x2:%d" % irel, "WV"))
            if irel < 2:
                i2 = 4 * k + irel + 2
                op("scalar", lambda e, i2=i2, xb=irel % 2: e.dma_start(
                    xld[xb][:], dx[i2 * 128:(i2 + 1) * 128, :]), (), ("xld%d" % (irel % 2),), dma=True)
            if irel >= 1:
                emit_ln3_stats(irel - 1)
            if irel >= 2:
                emit_ln3_tp(irel - 2)
        emit_ln3_stats(3)
        emit_ln3_tp(2)
        emit_ln3_tp(3)
        for rc in range(2):
            for c in range(NC):
                op("tensor", lambda e, rc=rc, c=c: e.matmul(
                    pG3[:], U1bf[:, c * 256 + rc * 128: c * 256 + (rc + 1) * 128],
                    xnt[:, c * 512:(c + 1) * 512],
                    start=(c == 0), stop=(c == NC - 1)), ("U1bf", "XNT"), ("pG3",))
            op("vector", lambda e, rc=rc: e.tensor_copy(PTb[:, rc * 512:(rc + 1) * 512], pG3[:]),
               ("pG3",), ("PTb", "wbf0"))
        def emit_u2(f):
            vb = f % 5
            hb = f % 3
            ku = "U2t%d" % vb
            kh = "Hbf%d" % hb if hb < 2 else "Hbf2"
            rks = (ku, kh, "OTp7") if hb == 2 else (ku, kh)
            op("tensor", lambda e, f=f, vb=vb, hb=hb: e.matmul(
                pG1, U2t[vb][:, 0:128], Hbf[hb][:],
                start=(f == 0), stop=(f == NF - 1)), rks, ("pG1",))
            op("tensor", lambda e, f=f, vb=vb, hb=hb: e.matmul(
                pG2, U2t[vb][:, 128:256], Hbf[hb][:],
                start=(f == 0), stop=(f == NF - 1)), rks, ("pG2",))

        def emit_ffn_loads(f):
            vb = f % 5
            kv, ku = "V1t%d" % vb, "U2t%d" % vb
            op("sync", lambda e, f=f, vb=vb: e.dma_start(
                V1t[vb][:].rearrange("p (rc c) -> p rc c", rc=2), dV1i[f].rearrange("rc p c -> p rc c")),
                ("V1I",), (kv,), dma=True)
            op("gpsimd", lambda e, f=f, vb=vb: e.dma_start(U2t[vb][:], dU2i[f]),
               ("U2I",), (ku,), dma=True)

        for q in range(4):
            emit_ffn_loads(q)
        for f in range(NF):
            vb = f % 5
            hb = f % 3
            kv, ku = "V1t%d" % vb, "U2t%d" % vb
            kh = "Hbf%d" % hb if hb < 2 else "Hbf2"
            hp, khp = [(pG3[:], "pG3"), (pS[:, 0:512], "pSa"), (pS[:, 512:1024], "pSb")][f % 3]
            op("tensor", lambda e, vb=vb, hp=hp: e.matmul(hp, V1t[vb][:, 0:128], PTb[:, 0:512],
                                                   start=True, stop=False), (kv, "PTb"), (khp,))
            op("tensor", lambda e, vb=vb, hp=hp: e.matmul(hp, V1t[vb][:, 128:256], PTb[:, 512:1024],
                                                   start=False, stop=True), (kv, "PTb"), (khp,))
            if f >= 1:
                emit_u2(f - 1)
            op("scalar", lambda e, f=f, hb=hb, hp=hp: e.activation(
                Hbf[hb][:], hp, AF.Gelu_apprx_tanh, bias=b1s[:, f:f + 1]),
                (khp, "b1s"), (kh, "OTp7") if hb == 2 else (kh,))
            if f + 4 < NF:
                emit_ffn_loads(f + 4)
        emit_u2(NF - 1)
        op("vector", lambda e: e.tensor_copy(QFT[:, 0:512], pG1), ("pG1",), ("QFT", "wbf1"))
        op("vector", lambda e: e.tensor_copy(QFT[:, 512:1024], pG2), ("pG2",), ("QFT", "wbf1"))
        v2banks = [(pG3[:], "pG3", False), (pO[:, 0:512], "pOh0", True),
                   (pO[:, 512:1024], "pOh1", True)]
        lastslot = {}
        for s in range(8):
            lastslot[v2banks[s % 3][1]] = s
        for irel in range(4):
            i = 4 * k + irel
            vps = []
            for eh in range(2):
                vp, kvp, ispo = v2banks[(2 * irel + eh) % 3]
                wk = (kvp, "pO") if ispo else (kvp,)
                op("tensor", lambda e, irel=irel, eh=eh, vp=vp: e.matmul(
                    vp, QFT[:, irel * 128:(irel + 1) * 128],
                    V2bf[:, eh * 512: (eh + 1) * 512], start=True, stop=False),
                    ("QFT", "V2bf"), wk)
                op("tensor", lambda e, irel=irel, eh=eh, vp=vp: e.matmul(
                    vp, QFT[:, 512 + irel * 128: 512 + (irel + 1) * 128],
                    V2bf[:, 1024 + eh * 512: 1024 + (eh + 1) * 512], start=False, stop=False),
                    ("QFT", "V2bf"), wk)
                op("tensor", lambda e, eh=eh, vp=vp: e.matmul(
                    vp, or_bf[:], b2row[0:1, eh * 512:(eh + 1) * 512],
                    start=False, stop=True), ("or_bf", "b2row"), wk)
                warm()
                vps.append((vp, kvp, ispo and (2 * irel + eh) == lastslot[kvp]))
            bi = 4 + irel
            st, ag, lnv, rs = stb[bi], agb[bi], lnvb[bi], rsb[bi]
            kst, kag, klnv, krs = "st%d" % bi, "ag%d" % bi, "lnv%d" % bi, "rs%d" % bi
            op("vector", lambda e, st=st, vp=vps[0][0]: e.bn_stats(st[:, 0:6], vp),
               (vps[0][1],), (kst,))
            op("vector", lambda e, st=st, vp=vps[1][0]: e.bn_stats(st[:, 6:12], vp),
               (vps[1][1],), (kst,))
            op("vector", lambda e, st=st, ag=ag: e.bn_aggr(ag[:], st[:]), (kst,), (kag,))
            op("scalar", lambda e, ag=ag, lnv=lnv: e.activation(lnv[:], ag[:, 1:2],
               AF.Ln, bias=scal[:, 0:1]), (kag, "scal"), (klnv,))
            op("scalar", lambda e, lnv=lnv, rs=rs: e.activation(rs[:], lnv[:], AF.Exp, scale=-0.5),
               (klnv,), (krs,))
            op("vector", lambda e, ag=ag, rs=rs, lnv=lnv: e.scalar_tensor_tensor(
                lnv[:], ag[:, 0:1], -1.0, rs[:], ALU.mult, ALU.mult),
                (kag, krs), (klnv,))
            for eh in range(2):
                rkeys = (vps[eh][1], krs, klnv) + (("pO",) if vps[eh][2] else ())
                op("scalar", lambda e, eh=eh, vp=vps[eh][0], rs=rs, lnv=lnv: e.activation(
                    fftmp[:, eh * 512:(eh + 1) * 512], vp, AF.Identity,
                    bias=lnv[:], scale=rs[:]),
                    rkeys, ("fftmp", "WSTG"))
            op("vector", lambda e, irel=irel: e.tensor_add(
                x2sb[:, irel * D:(irel + 1) * D], x2sb[:, irel * D:(irel + 1) * D], fftmp[:]),
                ("fftmp", "x2:%d" % irel), ("x2:%d" % irel, "WV"))
            op("scalar", lambda e, i=i, irel=irel: e.dma_start(
                dy[i * 128:(i + 1) * 128, :], x2sb[:, irel * D:(irel + 1) * D]),
                ("x2:%d" % irel,), ("Y",), dma=True)

    n = len(OPS)
    last_writer = {}
    readers_since = {}
    deps = [set() for _ in range(n)]
    for i, o in enumerate(OPS):
        for r in o["r"]:
            if r in last_writer:
                deps[i].add(last_writer[r])
        for w in o["w"]:
            if w in last_writer:
                deps[i].add(last_writer[w])
            for rd in readers_since.get(w, ()):
                deps[i].add(rd)
        for r in o["r"]:
            readers_since.setdefault(r, []).append(i)
        for w in o["w"]:
            last_writer[w] = i
            readers_since[w] = []
    for i, o in enumerate(OPS):
        deps[i].discard(i)
        drop = set()
        for j in deps[i]:
            if OPS[j]["e"] == o["e"]:
                if o["e"] == "tensor":
                    drop.add(j)
                elif not SAME_ENGINE_SYNC:
                    drop.add(j)
        deps[i] -= drop

    need_inc = set()
    for i in range(n):
        for j in deps[i]:
            need_inc.add(j)
    slot_tot = {}
    ordinal = [0] * n
    for i, o in enumerate(OPS):
        if o.get("dma"):
            slot_tot[o["slot"]] = slot_tot.get(o["slot"], 0) + 16
            ordinal[i] = slot_tot[o["slot"]]
            need_inc.add(i)
    cum = {}
    inc_after = [None] * n
    for i, o in enumerate(OPS):
        if not o.get("dma") and i in need_inc:
            cum[o["e"]] = cum.get(o["e"], 0) + 1
        inc_after[i] = cum.get(o["e"], 0)

    sem_names = ["sync", "tensor", "scalar", "vector", "gpsimd"]
    waits = [[] for _ in range(n)]
    for i in range(n):
        best = {}
        for j in deps[i]:
            if OPS[j].get("dma"):
                key = ("D", OPS[j]["slot"])
                v = ordinal[j]
            else:
                key = ("E", OPS[j]["e"])
                v = inc_after[j]
            if best.get(key, 0) < v:
                best[key] = v
        waits[i] = sorted(best.items())

    import contextlib
    with contextlib.ExitStack() as stack:
        sems = {("E", e): stack.enter_context(nc.semaphore("sem_" + e)) for e in sem_names}
        for sl in slot_tot:
            sems[("D", sl)] = stack.enter_context(nc.semaphore("dsem_" + sl))
        block = stack.enter_context(nc.Block())
        by_eng = {e: [i for i in range(n) if OPS[i]["e"] == e] for e in sem_names}
        final_counts = {("E", e): cum.get(e, 0) for e in sem_names}
        for sl, tot in slot_tot.items():
            final_counts[("D", sl)] = tot

        def make_body(en):
            def body(eng):
                hw = {}
                for i in by_eng[en]:
                    o = OPS[i]
                    for se, val in waits[i]:
                        if hw.get(se, 0) < val:
                            eng.wait_ge(sems[se], val)
                            hw[se] = val
                    inst = o["f"](eng)
                    if o.get("dma"):
                        inst.then_inc(sems[("D", o["slot"])], 16)
                    elif i in need_inc:
                        inst.then_inc(sems[("E", en)], 1)
                if en == "gpsimd":
                    for se, v in final_counts.items():
                        if v > 0:
                            eng.wait_ge(sems[se], v)
            return body

        for en in sem_names:
            getattr(block, en)(make_body(en))
    return nc


def kernel(**inputs):
    x = np.ascontiguousarray(inputs["L0_attn_x"], dtype=np.float32)
    weights = {
        "Wq": inputs["L0_attn_Wq"], "Wk": inputs["L0_attn_Wk"], "Wv": inputs["L0_attn_Wv"],
        "bq": inputs["L0_attn_bq"], "bk": inputs["L0_attn_bk"], "bv": inputs["L0_attn_bv"],
        "Wo": inputs["L0_attn_Wo"], "bo": inputs["L0_attn_bo"],
        "U1": inputs["L0_ffn_U1"], "V1": inputs["L0_ffn_V1"], "b1": inputs["L0_ffn_b1"],
        "U2": inputs["L0_ffn_U2"], "V2": inputs["L0_ffn_V2"], "b2": inputs["L0_ffn_b2"],
    }
    weights = {k: np.ascontiguousarray(v, dtype=np.float32) for k, v in weights.items()}
    nc = build_program()
    in_maps = [dict(weights, x=np.ascontiguousarray(x[i])) for i in range(8)]
    res = run_bass_kernel_spmd(nc, in_maps, core_ids=list(range(8)))
    return np.stack([res.results[i]["y"] for i in range(8)], axis=0)


if __name__ == "__main__":
    rng = np.random.default_rng(0)
    print("building program ...")
    nc = build_program()
    print("built ok")


# version 55 (final)
# speedup vs baseline: 1.0222x; 1.0222x over previous
import math
import numpy as np
import ml_dtypes
import concourse.bass as bass
import concourse.mybir as mybir
from concourse.bass_utils import run_bass_kernel_spmd

F32 = mybir.dt.float32
BF16 = mybir.dt.bfloat16
AF = mybir.ActivationFunctionType
ALU = mybir.AluOpType
AX = mybir.AxisListType

S = 2048
D = 1024
NH = 16
DH = 64
NP = NH // 2
NT = S // 128
NC = D // 128
NB = 4
WS = 512
RK = 256
DF = 4096
NF = DF // 128
EPS = 1e-5

SAME_ENGINE_SYNC = True


def build_program(debug=False):
    nc = bass.Bass("TRN2", target_bir_lowering=False)

    dx = nc.dram_tensor("x", [S, D], F32, kind="ExternalInput").ap()
    dWq = nc.dram_tensor("Wq", [NH, D, DH], F32, kind="ExternalInput").ap()
    dWk = nc.dram_tensor("Wk", [NH, D, DH], F32, kind="ExternalInput").ap()
    dWv = nc.dram_tensor("Wv", [NH, D, DH], F32, kind="ExternalInput").ap()
    dbq = nc.dram_tensor("bq", [NH, DH], F32, kind="ExternalInput").ap()
    dbk = nc.dram_tensor("bk", [NH, DH], F32, kind="ExternalInput").ap()
    dbv = nc.dram_tensor("bv", [NH, DH], F32, kind="ExternalInput").ap()
    dWo = nc.dram_tensor("Wo", [D, D], F32, kind="ExternalInput").ap()
    dbo = nc.dram_tensor("bo", [D], F32, kind="ExternalInput").ap()
    dU1 = nc.dram_tensor("U1", [D, RK], F32, kind="ExternalInput").ap()
    dV1 = nc.dram_tensor("V1", [RK, DF], F32, kind="ExternalInput").ap()
    db1 = nc.dram_tensor("b1", [DF], F32, kind="ExternalInput").ap()
    dU2 = nc.dram_tensor("U2", [DF, RK], F32, kind="ExternalInput").ap()
    dV2 = nc.dram_tensor("V2", [RK, D], F32, kind="ExternalInput").ap()
    db2 = nc.dram_tensor("b2", [D], F32, kind="ExternalInput").ap()
    dy = nc.dram_tensor("y", [S, D], F32, kind="ExternalOutput").ap()
    dV1i = nc.dram_tensor("V1i", [NF, 2, 128, 128], BF16, kind="Internal").ap()
    dU2i = nc.dram_tensor("U2i", [NF, 128, RK], BF16, kind="Internal").ap()

    dbg = {}
    def dbg_out(name, shape):
        if debug:
            dbg[name] = nc.dram_tensor("dbg_" + name, shape, F32, kind="ExternalOutput").ap()

    bf = ml_dtypes.bfloat16
    tri = np.triu(np.ones((128, 128))).astype(bf)
    c_ident = nc.inline_tensor(np.eye(128).astype(bf), "c_ident").ap()
    c_tri2 = nc.inline_tensor(np.concatenate([tri, tri], axis=1), "c_tri2").ap()
    c_or_bf = nc.inline_tensor(np.ones((1, 128), dtype=bf), "c_or_bf").ap()
    c_or_f32 = nc.inline_tensor(np.ones((1, 128), dtype=np.float32), "c_or_f32").ap()
    c_oc_bf = nc.inline_tensor(np.ones((128, 1), dtype=bf), "c_oc_bf").ap()
    c_oc_f32 = nc.inline_tensor(np.ones((128, 1), dtype=np.float32), "c_oc_f32").ap()
    _sc = np.zeros((128, 2), dtype=np.float32)
    _sc[:, 0] = EPS
    _sc[:, 1] = -math.log(8.0)
    c_scal = nc.inline_tensor(_sc, "c_scal").ap()

    A = nc.alloc_sbuf_tensor
    xnt = A("xnt", [128, NC * S], BF16)
    xnt_addr = nc.lookup_mloc(xnt).addr
    wobf = nc.alloc_sbuf_tensor_at("wobf", [128, 8192], BF16, offset=xnt_addr + 8192 * 2)
    QT = A("QT", [128, NP * S], BF16)
    KT = A("KT", [128, NP * S], BF16)
    Vst = A("Vst", [128, NT * 1056 + 64], BF16)
    x2sb = A("x2sb", [128, 4 * D], F32)
    x2_addr = nc.lookup_mloc(x2sb).addr
    wv_all = nc.alloc_sbuf_tensor_at("wv_all", [128, 8192], BF16, offset=x2_addr)
    OT = A("OT", [128, NC * WS], BF16)
    pt = [A("pt0", [128, 1024], BF16), A("pt1", [128, 1024], BF16), None]
    xn1tb = A("xn1tb", [128, 2 * D], BF16)
    xn1t = [xn1tb[:, 0:D], xn1tb[:, D:2 * D]]
    wstage2 = nc.alloc_sbuf_tensor_at("wstage2", [128, 1024], F32, offset=nc.lookup_mloc(xn1tb).addr)

    fftmp = A("fftmp", [128, D], F32)
    ff_addr = nc.lookup_mloc(fftmp).addr
    wstage = nc.alloc_sbuf_tensor_at("wstage", [128, 1024], F32, offset=ff_addr)
    xld = [A("xld0", [128, D], F32), A("xld1", [128, D], F32)]

    zbcP = A("zbcP", [128, 512], F32)
    lz = nc.alloc_sbuf_tensor_at("lz", [1, 1024], F32, offset=nc.lookup_mloc(xnt).addr + 8192)
    wbf = [A("wbf0", [128, 1024], BF16), A("wbf1", [128, 1024], BF16)]
    wbf0_addr = nc.lookup_mloc(wbf[0]).addr
    wbf1_addr = nc.lookup_mloc(wbf[1]).addr
    PTb = nc.alloc_sbuf_tensor_at("PTb", [128, 1024], BF16, offset=wbf0_addr)
    QFT = nc.alloc_sbuf_tensor_at("QFT", [128, 1024], BF16, offset=wbf1_addr)
    Hbf = [A("Hbf0", [128, 512], BF16), A("Hbf1", [128, 512], BF16), None]
    sq = nc.alloc_sbuf_tensor_at("sq", [128, 1024], BF16, offset=nc.lookup_mloc(OT).addr)
    sqB = nc.alloc_sbuf_tensor_at("sqB", [128, 1024], BF16, offset=nc.lookup_mloc(OT).addr + 2048)
    pt[2] = nc.alloc_sbuf_tensor_at("pt2", [128, 1024], BF16, offset=nc.lookup_mloc(xnt).addr + 12288)
    wst2 = nc.alloc_sbuf_tensor_at("wst2", [128, 512], F32, offset=nc.lookup_mloc(OT).addr + 4096)
    trfb = nc.alloc_sbuf_tensor_at("trfb", [128, 512], BF16, offset=nc.lookup_mloc(OT).addr + 6144)
    Hbf[2] = nc.alloc_sbuf_tensor_at("Hbf2", [128, 512], BF16, offset=nc.lookup_mloc(OT).addr + 7168)
    wst2b = A("wst2b", [128, 512], F32)
    trfbb = A("trfbb", [128, 512], BF16)
    brstage = nc.alloc_sbuf_tensor_at("brstage", [1, 1024], F32, offset=ff_addr)
    U1bf = A("U1bf", [128, 2048], BF16)
    V2bf = A("V2bf", [128, 2048], BF16)
    V1t = [A("V1t%d" % q, [128, 256], BF16) for q in range(5)]

    U2t = [A("U2t%d" % q, [128, 256], BF16) for q in range(5)]

    stb = [A("st%d" % q, [128, 12], F32) for q in range(8)]
    agb = [A("ag%d" % q, [128, 2], F32) for q in range(8)]
    lnvb = [A("lnv%d" % q, [128, 1], F32) for q in range(8)]
    rsb = [A("rs%d" % q, [128, 1], F32) for q in range(8)]
    qs1 = A("qs1", [128, 2], F32)
    st32 = A("st32", [128, 32], F32)
    t2b = A("t2b", [128, 32], F32)
    n8b = A("n8b", [128, 32], F32)
    btqk = A("btqk", [1, 16], F32)
    ltq = A("ltq", [1, 16], F32)
    lsq = A("lsq", [1, 8], F32)
    Brow = A("Brow", [1, 8], F32)
    nBrow = A("nBrow", [1, 8], F32)
    negBb = A("negBb", [128, 8], F32)
    bqs = A("bqs", [128, 8], F32)
    bks = A("bks", [128, 8], F32)
    b1s = A("b1s", [128, 32], F32)
    borow = A("borow", [1, 1024], BF16)
    b2row = A("b2row", [1, 1024], BF16)
    bvrow = A("bvrow", [1, 1024], BF16)

    ident = A("ident", [128, 128], BF16)
    tri2 = A("tri2", [128, 256], BF16)
    or_bf = A("or_bf", [1, 128], BF16)
    or_f32 = A("or_f32", [1, 128], F32)
    oc_bf = A("oc_bf", [128, 1], BF16)
    oc_f32 = A("oc_f32", [128, 1], F32)
    scal = A("scal", [128, 2], F32)

    P = nc.alloc_psum_tensor
    pS = P("pS", [128, 1024], F32)
    pO = P("pO", [128, 1024], F32)
    pG12 = P("pG12", [128, 1024], F32)
    pG1 = pG12[:, 0:512]
    pG2 = pG12[:, 512:1024]
    pG3 = P("pG3", [128, 512], F32)
    pTP = P("pTP", [128, 1024], BF16)

    OPS = []

    def op(engine, fn, reads=(), writes=(), dma=False, slot=None):
        OPS.append({"e": engine, "f": fn, "r": tuple(reads), "w": tuple(writes),
                    "inc": 16 if dma else 1, "dma": dma,
                    "slot": (slot or (writes[0] if writes else "misc")) if dma else None})

    def warm():
        op("tensor", lambda e: e.ldweights(ident[0:1, 0:1]), ("ident",), ())

    op("gpsimd", lambda e: e.dma_start(ident[:], c_ident), (), ("ident",), dma=True)
    op("gpsimd", lambda e: e.dma_start(tri2[:], c_tri2), (), ("tri2",), dma=True)
    op("gpsimd", lambda e: e.dma_start(or_bf[:], c_or_bf), (), ("or_bf",), dma=True)
    op("gpsimd", lambda e: e.dma_start(or_f32[:], c_or_f32), (), ("or_f32",), dma=True)
    op("gpsimd", lambda e: e.dma_start(oc_bf[:], c_oc_bf), (), ("oc_bf",), dma=True)
    op("gpsimd", lambda e: e.dma_start(oc_f32[:], c_oc_f32), (), ("oc_f32",), dma=True)
    op("gpsimd", lambda e: e.dma_start(scal[:], c_scal), (), ("scal",), dma=True)
    for nm, src, dst in (("bqs", dbq, bqs), ("bks", dbk, bks)):
        op("gpsimd", lambda e, s=src, d=dst: e.dma_start(
            d[:], s.rearrange("(p two) d -> (two d) p", two=2),
            allow_slow_non_contiguous=True), (), (nm,), dma=True)
    op("gpsimd", lambda e: e.dma_start(
        b1s[:], db1.rearrange("(c P) -> P c", P=128),
        allow_slow_non_contiguous=True), (), ("b1s",), dma=True)
    for nm, src, dst in (("borow", dbo, borow), ("b2row", db2, b2row)):
        op("gpsimd", lambda e, s=src: e.dma_start(brstage[:], s.rearrange("(a d) -> a d", a=1)),
           ("WSTG",), ("brstage", "WSTG"), dma=True)
        op("vector", lambda e, d=dst: e.tensor_copy(d[:], brstage[:]), ("brstage",), (nm,))
    op("gpsimd", lambda e: e.dma_start(brstage[:], dbv.rearrange("(a h) d -> a (h d)", a=1)),
       ("WSTG",), ("brstage", "WSTG"), dma=True)
    op("vector", lambda e: e.tensor_copy(bvrow[:], brstage[:]), ("brstage",), ("bvrow",))
    op("vector", lambda e: e.memset(Vst[:], 0.0), (), ("VST",))
    op("vector", lambda e: e.memset(
        Vst[:, 0:NT * 1056].rearrange("p (t g c) -> p t g c", t=NT, g=16)[:, :, :, 64:65], 1.0),
        ("VST",), ("VST",))

    def emit_ln_stats(src_tiles, bi, extra_reads):
        st, ag, lnv, rs = stb[bi], agb[bi], lnvb[bi], rsb[bi]
        kst, kag, klnv, krs = "st%d" % bi, "ag%d" % bi, "lnv%d" % bi, "rs%d" % bi
        op("vector", lambda e: e.bn_stats(st[:, 0:6], src_tiles[0]), extra_reads, (kst,))
        op("vector", lambda e: e.bn_stats(st[:, 6:12], src_tiles[1]), extra_reads, (kst,))
        op("vector", lambda e: e.bn_aggr(ag[:], st[:]), (kst,), (kag,))
        op("scalar", lambda e: e.activation(lnv[:], ag[:, 1:2], AF.Ln, bias=scal[:, 0:1]),
           (kag, "scal"), (klnv,))
        op("scalar", lambda e: e.activation(rs[:], lnv[:], AF.Exp, scale=-0.5), (klnv,), (krs,))
        warm()
        return ag, rs, kag, krs

    def emit_ln1_tp(i):
        nb = i % 2
        kn = "xn1t%d" % nb
        for c in range(NC):
            op("tensor", lambda e, nb=nb, c=c: e.transpose(
                pTP[:, c * 128:(c + 1) * 128], xn1t[nb][:, c * 128:(c + 1) * 128], ident[:]),
                (kn, "ident"), ("pTP",))
        if i % 2 == 1:
            op("scalar", lambda e, i=i: e.activation(
                xnt[:].rearrange("p (c s) -> p c s", c=NC)[:, :, i * 128:(i + 1) * 128],
                pTP[:].rearrange("p (c s) -> p c s", c=NC), AF.Copy),
                ("pTP",), ("XNT", "xnti%d" % i))
        else:
            op("vector", lambda e, i=i: e.tensor_copy(
                xnt[:].rearrange("p (c s) -> p c s", c=NC)[:, :, i * 128:(i + 1) * 128],
                pTP[:].rearrange("p (c s) -> p c s", c=NC)),
                ("pTP",), ("XNT", "xnti%d" % i))

    for i in range(NT):
        xb = i % 2
        kx = "xld%d" % xb
        xeng = "gpsimd" if i % 2 == 0 else "scalar"
        op(xeng, lambda e, i=i, xb=xb: e.dma_start(xld[xb][:], dx[i * 128:(i + 1) * 128, :]),
           (), (kx,), dma=True)
        bi = i % 4
        ag, rs, kag, krs = emit_ln_stats(
            [xld[xb][:, 0:512], xld[xb][:, 512:1024]], bi, (kx,))
        nb = i % 2
        kn = "xn1t%d" % nb
        if i % 2 == 1:
            lnv, klnv = lnvb[bi], "lnv%d" % bi
            op("vector", lambda e, ag=ag, rs=rs, lnv=lnv: e.scalar_tensor_tensor(
                lnv[:], ag[:, 0:1], -1.0, rs[:], ALU.mult, ALU.mult),
                (kag, krs), (klnv,))
            op("scalar", lambda e, xb=xb, nb=nb, rs=rs, lnv=lnv: e.activation(
                xn1t[nb], xld[xb][:], AF.Identity, bias=lnv[:], scale=rs[:]),
                (kx, krs, klnv), (kn,))
        else:
            op("vector", lambda e, xb=xb, nb=nb, ag=ag, rs=rs: e.tensor_scalar(
                xn1t[nb], xld[xb][:], ag[:, 0:1], rs[:], ALU.subtract, ALU.mult),
                (kx, kag, krs), (kn,))
        if i >= 1:
            emit_ln1_tp(i - 1)
    emit_ln1_tp(NT - 1)

    stage_cnt = [0]

    def stage_into(emitters, dstfn, kd, extra_w=()):
        par = stage_cnt[0] % 2
        stage_cnt[0] += 1
        ws = wstage if par == 0 else wstage2
        kws = "WSTG.%d" % par
        ex = ("WSTG",) if par == 0 else ("xn1t0", "xn1t1")
        for fn in emitters:
            op("sync", lambda e, fn=fn, ws=ws: fn(e, ws), (kws,), (kws,) + ex, dma=True)
        op("vector", lambda e, ws=ws: dstfn(e, ws), (kws,), (kd,) + ex + tuple(extra_w))

    def stage_w_pair(dW, p, dstbuf):
        kd = "wbf%d" % dstbuf
        stage_into(
            [lambda e, ws, dW=dW, p=p: e.dma_start(
                ws[:].rearrange("p (c d) -> p c d", c=NC)[:, :, 0:64],
                dW[2 * p].rearrange("(c P) d -> P c d", P=128)),
             lambda e, ws, dW=dW, p=p: e.dma_start(
                ws[:].rearrange("p (c d) -> p c d", c=NC)[:, :, 64:128],
                dW[2 * p + 1].rearrange("(c P) d -> P c d", P=128))],
            lambda e, ws, b=dstbuf: e.tensor_copy(wbf[b][:], ws[:]), kd)
        return kd

    def emit_transform_loads(f):
        par = f % 2
        w2 = wst2 if par == 0 else wst2b
        kw2v = "WST2V.%d" % par
        kw2u = "WST2U.%d" % par
        ex = ("OTp4", "OTp5") if par == 0 else ()
        op("gpsimd", lambda e, f=f, w2=w2: e.dma_start(
            w2[:, 0:256].rearrange("p (rc c) -> p rc c", rc=2),
            dV1[:, f * 128:(f + 1) * 128].rearrange("(rc P) c -> P rc c", P=128)),
            (kw2v,), (kw2v,) + ex, dma=True, slot="W2V%d" % par)
        op("gpsimd", lambda e, f=f, w2=w2: e.dma_start(w2[:, 256:512], dU2[f * 128:(f + 1) * 128, :]),
           (kw2u,), (kw2u,) + ex, dma=True, slot="W2U%d" % par)

    def emit_transform_rest(f):
        par = f % 2
        w2, tb = (wst2, trfb) if par == 0 else (wst2b, trfbb)
        kw2v = "WST2V.%d" % par
        kw2u = "WST2U.%d" % par
        ktbv = "TRFBV.%d" % par
        ktbu = "TRFBU.%d" % par
        ex = ("OTp4", "OTp5", "OTp6") if par == 0 else ()
        op("gpsimd", lambda e, w2=w2, tb=tb: e.tensor_copy(tb[:, 0:256], w2[:, 0:256]),
           (kw2v,), (ktbv, kw2v) + ex)
        op("gpsimd", lambda e, f=f, tb=tb: e.dma_start(
            dV1i[f].rearrange("rc p c -> p rc c"),
            tb[:, 0:256].rearrange("p (rc c) -> p rc c", rc=2)),
            (ktbv,) + ex, ("V1I",), dma=True, slot="V1I")
        op("gpsimd", lambda e, w2=w2, tb=tb: e.tensor_copy(tb[:, 256:512], w2[:, 256:512]),
           (kw2u,), (ktbu, kw2u) + ex)
        op("gpsimd", lambda e, f=f, tb=tb: e.dma_start(dU2i[f], tb[:, 256:512]),
           (ktbu,) + ex, ("U2I",), dma=True, slot="U2I")

    for p in range(NP):
        for f in range(4 * p, 4 * p + 4):
            emit_transform_loads(f)
            if f >= 1:
                emit_transform_rest(f - 1)
        for proj, dW, dst, kdst, bias in (("q", dWq, QT, "QTm", bqs), ("k", dWk, KT, "KTm", bks)):
            wb = (2 * p + (0 if proj == "q" else 1)) % 2
            kw = stage_w_pair(dW, p, wb)
            kbias = "bqs" if proj == "q" else "bks"
            for b in range(NB):
                pg, kpg = (pG1, "pG1") if b % 2 == 0 else (pG2, "pG2")
                for c in range(NC):
                    op("tensor", lambda e, wb=wb, c=c, b=b, p=p, pg=pg: e.matmul(
                        pg[:], wbf[wb][:, c * 128:(c + 1) * 128],
                        xnt[:, c * S + b * WS: c * S + (b + 1) * WS],
                        start=(c == 0), stop=(c == NC - 1)),
                        (kw, "XNT"), (kpg,))
                op("scalar", lambda e, p=p, b=b, dst=dst, bias=bias, pg=pg: e.activation(
                    dst[:, p * S + b * WS: p * S + (b + 1) * WS], pg[:], AF.Identity,
                    bias=bias[:, p:p + 1]), (kpg, kbias), (kdst,))

    emit_transform_rest(NF - 1)
    for c in range(NC):
        for o in range(2):
            stage_into(
                [lambda e, ws, c=c, o=o: e.dma_start(
                    ws[:, 0:512].rearrange("p (h d) -> p h d", h=8),
                    dWv[8 * o:8 * o + 8, c * 128:(c + 1) * 128, :].rearrange("h P d -> P h d"))],
                lambda e, ws, c=c, o=o: e.tensor_copy(
                    wv_all[:, c * 1024 + o * 512: c * 1024 + (o + 1) * 512], ws[:, 0:512]), "WV")
    for i in range(NT):
        for o in range(2):
            pg, kpg = (pG2, "pG2") if (2 * i + o) % 2 == 0 else (pG3, "pG3")
            for c in range(NC):
                op("tensor", lambda e, i=i, o=o, c=c, pg=pg: e.matmul(
                    pg[:], xnt[:, c * S + i * 128: c * S + (i + 1) * 128],
                    wv_all[:, c * 1024 + o * 512: c * 1024 + (o + 1) * 512],
                    start=(c == 0), stop=False), ("XNT", "WV"), (kpg,))
            op("tensor", lambda e, o=o, pg=pg: e.matmul(
                pg[:], or_bf[:], bvrow[0:1, o * 512:(o + 1) * 512],
                start=False, stop=True), ("or_bf", "bvrow"), (kpg,))
            op("vector", lambda e, i=i, o=o, pg=pg: e.tensor_copy(
                Vst[:, i * 1056 + (4 * o) * 132: i * 1056 + (4 * o + 4) * 132]
                   .rearrange("p (pr h d) -> p pr h d", pr=4, h=2)[:, :, :, 0:64],
                pg[:].rearrange("p (pr h d) -> p pr h d", pr=4, h=2)),
                (kpg,), ("VST",))

    for p in range(NP):
        for ti, (src, kt) in enumerate(((QT, "QTm"), (KT, "KTm"))):
            for half in range(2):
                sqb, ksq = (sq, "sq") if (2 * ti + half) % 2 == 0 else (sqB, "sqB")
                kex = ("OTp0", "OTp1") if (2 * ti + half) % 2 == 0 else ("OTp2", "OTp3")
                op("vector", lambda e, src=src, p=p, half=half, sqb=sqb: e.tensor_mul(
                    sqb[:], src[:, p * S + half * 1024: p * S + (half + 1) * 1024],
                    src[:, p * S + half * 1024: p * S + (half + 1) * 1024]), (kt,), (ksq,) + kex)
                for j in range(8):
                    col = ti * 16 + half * 8 + j
                    op("tensor", lambda e, j=j, col=col, sqb=sqb: e.matmul(
                        pG3[:, col:col + 1], sqb[:, j * 128:(j + 1) * 128], oc_bf[:],
                        start=True, stop=True), (ksq, "oc_bf"), ("pG3",))
        op("vector", lambda e: e.tensor_copy(st32[:], pG3[:, 0:32]), ("pG3",), ("st32",))
        op("vector", lambda e: e.tensor_mul(t2b[:], st32[:], st32[:]), ("st32",), ("t2b",))
        op("vector", lambda e: e.tensor_mul(n8b[:], t2b[:], t2b[:]), ("t2b",), ("n8b",))
        op("vector", lambda e: e.tensor_reduce(qs1[:, 0:1], n8b[:, 0:16], AX.X, ALU.add),
           ("n8b",), ("qs1",))
        op("vector", lambda e: e.tensor_reduce(qs1[:, 1:2], n8b[:, 16:32], AX.X, ALU.add),
           ("n8b",), ("qs1",))
        op("tensor", lambda e, p=p: e.matmul(pG3[0:1, 40 + p:41 + p], qs1[:, 0:1], oc_f32[:],
                                             start=True, stop=True), ("qs1", "oc_f32"), ("pG3",))
        op("tensor", lambda e, p=p: e.matmul(pG3[0:1, 48 + p:49 + p], qs1[:, 1:2], oc_f32[:],
                                             start=True, stop=True), ("qs1", "oc_f32"), ("pG3",))
    op("vector", lambda e: e.tensor_copy(btqk[:], pG3[0:1, 40:56]), ("pG3",), ("btqk",))
    op("scalar", lambda e: e.activation(ltq[:], btqk[:], AF.Ln), ("btqk",), ("ltq",))
    op("vector", lambda e: e.tensor_add(lsq[:], ltq[0:1, 0:8], ltq[0:1, 8:16]), ("ltq",), ("lsq",))
    op("scalar", lambda e: e.activation(Brow[:], lsq[:], AF.Exp, scale=0.125,
                                        bias=scal[0:1, 1:2]), ("lsq", "scal"), ("Brow",))
    op("vector", lambda e: e.tensor_scalar_mul(nBrow[:], Brow[:], -1.0), ("Brow",), ("nBrow",))
    for p in range(NP):
        op("tensor", lambda e, p=p: e.matmul(pG3[:, 64 + p:65 + p], or_f32[:],
                                             nBrow[0:1, p:p + 1], start=True, stop=True),
           ("nBrow", "or_f32"), ("pG3",))
    op("vector", lambda e: e.tensor_copy(negBb[:], pG3[:, 64:72]), ("pG3",), ("negBb",))

    for c in range(NC):
        stage_into(
            [lambda e, ws, c=c: e.dma_start(ws[:], dWo[c * 128:(c + 1) * 128, :])],
            lambda e, ws, c=c: e.tensor_copy(wobf[:, c * 1024:(c + 1) * 1024], ws[:]),
            "wobf%d" % c, extra_w=("XNT",))
    for half in range(2):
        stage_into(
            [lambda e, ws, half=half: e.dma_start(
                ws[:].rearrange("p (c r) -> p c r", c=4),
                dU1[half * 512:(half + 1) * 512, :].rearrange("(c P) r -> P c r", P=128))],
            lambda e, ws, half=half: e.tensor_copy(
                U1bf[:, half * 1024:(half + 1) * 1024], ws[:]), "U1bf")
    for rc in range(2):
        stage_into(
            [lambda e, ws, rc=rc: e.dma_start(ws[:], dV2[rc * 128:(rc + 1) * 128, :])],
            lambda e, ws, rc=rc: e.tensor_copy(V2bf[:, rc * 1024:(rc + 1) * 1024], ws[:]), "V2bf")

    gchunk = [0]
    for k in range(NB):
        ntc = 4 * (k + 1)
        for pre_i in (0, 1):
            i0 = 4 * k + pre_i
            op("scalar", lambda e, i0=i0, xb=pre_i % 2: e.dma_start(
                xld[xb][:], dx[i0 * 128:(i0 + 1) * 128, :]), (), ("xld%d" % (pre_i % 2),), dma=True)

        def make_epilogue(p):
            def emit_epi1():
                op("vector", lambda e, p=p: e.tensor_copy(
                    OT[0:64, p * 512:(p + 1) * 512], pO[0:64, 0:512]),
                    ("pO",), ("OTm", "OTp%d" % p))
                op("vector", lambda e, p=p: e.tensor_copy(
                    OT[64:128, p * 512:(p + 1) * 512], pO[0:64, 512:1024]),
                    ("pO",), ("OTm", "OTp%d" % p))
                op("scalar", lambda e: e.activation(lz[:], pO[64:65, :], AF.Ln),
                   ("pO",), ("lz", "XNT"))
                warm()

            def emit_epi2():
                op("tensor", lambda e: e.matmul(pG3[0:64, :], or_f32[0:1, 0:64], lz[0:1, 0:512],
                                                start=True, stop=True), ("lz", "or_f32"), ("pG3",))
                op("tensor", lambda e: e.matmul(pG3[64:128, :], or_f32[0:1, 0:64], lz[0:1, 512:1024],
                                                start=True, stop=True), ("lz", "or_f32"), ("pG3",))
                op("scalar", lambda e: e.activation(zbcP[:], pG3[:], AF.Exp, scale=-1.0),
                   ("pG3",), ("zbcP",))
                op("vector", lambda e, p=p: e.tensor_mul(
                    OT[0:64, p * 512:(p + 1) * 512],
                    OT[0:64, p * 512:(p + 1) * 512], zbcP[0:64, :]),
                    ("zbcP", "OTm", "OTp%d" % p), ("OTm", "OTp%d" % p))
                op("vector", lambda e, p=p: e.tensor_mul(
                    OT[64:128, p * 512:(p + 1) * 512],
                    OT[64:128, p * 512:(p + 1) * 512], zbcP[64:128, :]),
                    ("zbcP", "OTm", "OTp%d" % p), ("OTm", "OTp%d" % p))
            return emit_epi1, emit_epi2

        def emit_pv(p, info, first, last):
            j, off, width, pb = info
            kpt = "pt%d" % pb
            for h in range(2):
                op("tensor", lambda e, p=p, j=j, h=h, off=off, width=width, pb=pb,
                   st=first, sp=last: e.matmul(
                    pO[:, h * 512 + off: h * 512 + off + width],
                    Vst[:, j * 1056 + p * 132 + h * 66: j * 1056 + p * 132 + h * 66 + 128],
                    pt[pb][:, h * 512 + off: h * 512 + off + width],
                    start=st, stop=sp), (kpt, "VST"), ("pO",))

        pending_epi = None
        for p in range(NP):
            pend_pv = []
            for j in range(ntc):
                r = j - 4 * k
                off = 128 * r if r >= 0 else 0
                width = WS - off
                pb = gchunk[0] % 3
                sparity = gchunk[0] % 2
                gchunk[0] += 1
                kpt = "pt%d" % pb
                kptw = (kpt, "XNT") if pb == 2 else (kpt,)
                if sparity == 0:
                    svs = [(pS[:, off:off + width], "pSa"),
                           (pS[:, 512 + off: 512 + off + width], "pSb")]
                else:
                    svs = [(pG1[:, off:off + width], "pG1"),
                           (pG2[:, off:off + width], "pG2")]
                for h in range(2):
                    op("tensor", lambda e, p=p, j=j, h=h, off=off, width=width, k=k,
                       sv=svs[h][0]: e.matmul(
                        sv,
                        KT[h * 64:(h + 1) * 64, p * S + j * 128: p * S + (j + 1) * 128],
                        QT[h * 64:(h + 1) * 64, p * S + k * WS + off: p * S + k * WS + off + width],
                        start=True, stop=True, tile_position=(64 * h, 0)),
                        ("QTm", "KTm"), (svs[h][1],))
                if j == 1 and pending_epi is not None:
                    pending_epi[0]()
                if j == 3 and pending_epi is not None:
                    pending_epi[1]()
                    pending_epi = None
                if len(pend_pv) >= 2:
                    info = pend_pv.pop(0)
                    emit_pv(p, info, first=(info[0] == 0), last=False)
                if sparity == 0:
                    op("scalar", lambda e, p=p, pb=pb, off=off, width=width: e.activation(
                        pt[pb][:].rearrange("p (h s) -> p h s", h=2)[:, :, off:off + width],
                        pS[:].rearrange("p (h s) -> p h s", h=2)[:, :, off:off + width],
                        AF.Exp, bias=negBb[:, p:p + 1], scale=0.125),
                        ("pSa", "pSb", "negBb"), kptw)
                else:
                    op("scalar", lambda e, p=p, pb=pb, off=off, width=width: e.activation(
                        pt[pb][:].rearrange("p (h s) -> p h s", h=2)[:, :, off:off + width],
                        pG12[:].rearrange("p (h s) -> p h s", h=2)[:, :, off:off + width],
                        AF.Exp, bias=negBb[:, p:p + 1], scale=0.125),
                        ("pG1", "pG2", "negBb"), kptw)
                if r >= 0:
                    op("vector", lambda e, pb=pb, off=off: e.tensor_mul(
                        pt[pb][:].rearrange("p (h s) -> p h s", h=2)[:, :, off:off + 128],
                        pt[pb][:].rearrange("p (h s) -> p h s", h=2)[:, :, off:off + 128],
                        tri2[:].rearrange("p (h s) -> p h s", h=2)),
                        (kpt, "tri2"), kptw)
                pend_pv.append((j, off, width, pb))
                warm()
            if pending_epi is not None:
                pending_epi()
                pending_epi = None
            while pend_pv:
                info = pend_pv.pop(0)
                emit_pv(p, info, first=(info[0] == 0), last=(len(pend_pv) == 0))
            pending_epi = make_epilogue(p)
        pending_epi[0]()
        pending_epi[1]()
        pending_epi = None
        def emit_ln3_stats(irel):
            bi3 = 4 + irel
            kx2 = "x2:%d" % irel
            ag3, rs3, kag3, krs3 = emit_ln_stats(
                [x2sb[:, irel * D: irel * D + 512], x2sb[:, irel * D + 512: (irel + 1) * D]],
                bi3, (kx2,))
            kn3 = "xn1t%d" % (irel % 2)
            op("vector", lambda e, irel=irel, nb=irel % 2, ag=ag3, rs=rs3: e.tensor_scalar(
                xn1t[nb], x2sb[:, irel * D:(irel + 1) * D], ag[:, 0:1], rs[:],
                ALU.subtract, ALU.mult), (kx2, kag3, krs3), (kn3,))

        def emit_ln3_tp(irel):
            kn3 = "xn1t%d" % (irel % 2)
            for c in range(NC):
                op("tensor", lambda e, nb=irel % 2, c=c: e.transpose(
                    pTP[:, c * 128:(c + 1) * 128], xn1t[nb][:, c * 128:(c + 1) * 128], ident[:]),
                    (kn3, "ident"), ("pTP",))
            op("scalar", lambda e, irel=irel: e.activation(
                xnt[:, 0:4096].rearrange("p (c s) -> p c s", c=NC)[:, :, irel * 128:(irel + 1) * 128],
                pTP[:].rearrange("p (c s) -> p c s", c=NC), AF.Copy),
                ("pTP",), ("XNT",) + tuple("xnti%d" % q for q in range(16)))
            warm()

        for irel in range(4):
            i = 4 * k + irel
            xb = i % 2
            kx = "xld%d" % xb
            if irel % 2 == 0:
                ehbanks = ((0, pG1, "pG1"), (1, pG2, "pG2"))
            else:
                ehbanks = ((0, pS[:, 0:512], "pSa"), (1, pS[:, 512:1024], "pSb"))
            for eh, pg, kpg in ehbanks:
                for c in range(NC):
                    op("tensor", lambda e, c=c, irel=irel, eh=eh, pg=pg: e.matmul(
                        pg, OT[:, c * 512 + irel * 128: c * 512 + (irel + 1) * 128],
                        wobf[:, c * 1024 + eh * 512: c * 1024 + (eh + 1) * 512],
                        start=(c == 0), stop=False), ("OTm", "wobf%d" % c), (kpg,))
                op("tensor", lambda e, eh=eh, pg=pg: e.matmul(
                    pg, or_bf[:], borow[0:1, eh * 512:(eh + 1) * 512],
                    start=False, stop=True), ("or_bf", "borow"), (kpg,))
                warm()
            bi = i % 4
            st, ag, lnv, rs = stb[bi], agb[bi], lnvb[bi], rsb[bi]
            kst, kag, klnv, krs = "st%d" % bi, "ag%d" % bi, "lnv%d" % bi, "rs%d" % bi
            op("vector", lambda e, st=st, pg=ehbanks[0][1], kpg=ehbanks[0][2]: e.bn_stats(
                st[:, 0:6], pg), (ehbanks[0][2],), (kst,))
            op("vector", lambda e, st=st, pg=ehbanks[1][1]: e.bn_stats(
                st[:, 6:12], pg), (ehbanks[1][2],), (kst,))
            op("vector", lambda e, st=st, ag=ag: e.bn_aggr(ag[:], st[:]), (kst,), (kag,))
            op("scalar", lambda e, ag=ag, lnv=lnv: e.activation(lnv[:], ag[:, 1:2],
               AF.Ln, bias=scal[:, 0:1]), (kag, "scal"), (klnv,))
            op("scalar", lambda e, lnv=lnv, rs=rs: e.activation(rs[:], lnv[:], AF.Exp, scale=-0.5),
               (klnv,), (krs,))
            op("vector", lambda e, ag=ag, rs=rs, lnv=lnv: e.scalar_tensor_tensor(
                lnv[:], ag[:, 0:1], -1.0, rs[:], ALU.mult, ALU.mult),
                (kag, krs), (klnv,))
            for eh, pg, kpg in ehbanks:
                op("scalar", lambda e, pg=pg, eh=eh, rs=rs, lnv=lnv: e.activation(
                    fftmp[:, eh * 512:(eh + 1) * 512], pg, AF.Identity,
                    bias=lnv[:], scale=rs[:]),
                    (kpg, krs, klnv), ("fftmp%d" % eh, "WSTG"))
                op("vector", lambda e, irel=irel, eh=eh, xb=xb: e.tensor_add(
                    x2sb[:, irel * D + eh * 512: irel * D + (eh + 1) * 512],
                    xld[xb][:, eh * 512:(eh + 1) * 512],
                    fftmp[:, eh * 512:(eh + 1) * 512]),
                    ("fftmp%d" % eh, kx), ("x2:%d" % irel, "WV"))
            if irel < 2:
                i2 = 4 * k + irel + 2
                op("scalar", lambda e, i2=i2, xb=irel % 2: e.dma_start(
                    xld[xb][:], dx[i2 * 128:(i2 + 1) * 128, :]), (), ("xld%d" % (irel % 2),), dma=True)
            if irel >= 1:
                emit_ln3_stats(irel - 1)
            if irel >= 2:
                emit_ln3_tp(irel - 2)
        emit_ln3_stats(3)
        emit_ln3_tp(2)
        emit_ln3_tp(3)
        for rc in range(2):
            for c in range(NC):
                op("tensor", lambda e, rc=rc, c=c: e.matmul(
                    pG3[:], U1bf[:, c * 256 + rc * 128: c * 256 + (rc + 1) * 128],
                    xnt[:, c * 512:(c + 1) * 512],
                    start=(c == 0), stop=(c == NC - 1)), ("U1bf", "XNT"), ("pG3",))
            op("vector", lambda e, rc=rc: e.tensor_copy(PTb[:, rc * 512:(rc + 1) * 512], pG3[:]),
               ("pG3",), ("PTb", "wbf0"))
        def emit_u2(f):
            vb = f % 5
            hb = f % 3
            ku = "U2t%d" % vb
            kh = "Hbf%d" % hb if hb < 2 else "Hbf2"
            rks = (ku, kh, "OTp7") if hb == 2 else (ku, kh)
            op("tensor", lambda e, f=f, vb=vb, hb=hb: e.matmul(
                pG1, U2t[vb][:, 0:128], Hbf[hb][:],
                start=(f == 0), stop=(f == NF - 1)), rks, ("pG1",))
            op("tensor", lambda e, f=f, vb=vb, hb=hb: e.matmul(
                pG2, U2t[vb][:, 128:256], Hbf[hb][:],
                start=(f == 0), stop=(f == NF - 1)), rks, ("pG2",))

        def emit_ffn_loads(f):
            vb = f % 5
            kv, ku = "V1t%d" % vb, "U2t%d" % vb
            op("sync", lambda e, f=f, vb=vb: e.dma_start(
                V1t[vb][:].rearrange("p (rc c) -> p rc c", rc=2), dV1i[f].rearrange("rc p c -> p rc c")),
                ("V1I",), (kv,), dma=True)
            op("gpsimd", lambda e, f=f, vb=vb: e.dma_start(U2t[vb][:], dU2i[f]),
               ("U2I",), (ku,), dma=True)

        for q in range(4):
            emit_ffn_loads(q)
        for f in range(NF):
            vb = f % 5
            hb = f % 3
            kv, ku = "V1t%d" % vb, "U2t%d" % vb
            kh = "Hbf%d" % hb if hb < 2 else "Hbf2"
            hp, khp = [(pG3[:], "pG3"), (pS[:, 0:512], "pSa"), (pS[:, 512:1024], "pSb")][f % 3]
            op("tensor", lambda e, vb=vb, hp=hp: e.matmul(hp, V1t[vb][:, 0:128], PTb[:, 0:512],
                                                   start=True, stop=False), (kv, "PTb"), (khp,))
            op("tensor", lambda e, vb=vb, hp=hp: e.matmul(hp, V1t[vb][:, 128:256], PTb[:, 512:1024],
                                                   start=False, stop=True), (kv, "PTb"), (khp,))
            if f >= 1:
                emit_u2(f - 1)
            op("scalar", lambda e, f=f, hb=hb, hp=hp: e.activation(
                Hbf[hb][:], hp, AF.Gelu_apprx_tanh, bias=b1s[:, f:f + 1]),
                (khp, "b1s"), (kh, "OTp7") if hb == 2 else (kh,))
            if f + 4 < NF:
                emit_ffn_loads(f + 4)
        emit_u2(NF - 1)
        op("vector", lambda e: e.tensor_copy(QFT[:, 0:512], pG1), ("pG1",), ("QFT", "wbf1"))
        op("vector", lambda e: e.tensor_copy(QFT[:, 512:1024], pG2), ("pG2",), ("QFT", "wbf1"))
        v2banks = [(pG3[:], "pG3", False), (pO[:, 0:512], "pOh0", True),
                   (pO[:, 512:1024], "pOh1", True)]
        lastslot = {}
        for s in range(8):
            lastslot[v2banks[s % 3][1]] = s
        for irel in range(4):
            i = 4 * k + irel
            vps = []
            for eh in range(2):
                vp, kvp, ispo = v2banks[(2 * irel + eh) % 3]
                wk = (kvp, "pO") if ispo else (kvp,)
                op("tensor", lambda e, irel=irel, eh=eh, vp=vp: e.matmul(
                    vp, QFT[:, irel * 128:(irel + 1) * 128],
                    V2bf[:, eh * 512: (eh + 1) * 512], start=True, stop=False),
                    ("QFT", "V2bf"), wk)
                op("tensor", lambda e, irel=irel, eh=eh, vp=vp: e.matmul(
                    vp, QFT[:, 512 + irel * 128: 512 + (irel + 1) * 128],
                    V2bf[:, 1024 + eh * 512: 1024 + (eh + 1) * 512], start=False, stop=False),
                    ("QFT", "V2bf"), wk)
                op("tensor", lambda e, eh=eh, vp=vp: e.matmul(
                    vp, or_bf[:], b2row[0:1, eh * 512:(eh + 1) * 512],
                    start=False, stop=True), ("or_bf", "b2row"), wk)
                warm()
                vps.append((vp, kvp, ispo and (2 * irel + eh) == lastslot[kvp]))
            bi = 4 + irel
            st, ag, lnv, rs = stb[bi], agb[bi], lnvb[bi], rsb[bi]
            kst, kag, klnv, krs = "st%d" % bi, "ag%d" % bi, "lnv%d" % bi, "rs%d" % bi
            op("vector", lambda e, st=st, vp=vps[0][0]: e.bn_stats(st[:, 0:6], vp),
               (vps[0][1],), (kst,))
            op("vector", lambda e, st=st, vp=vps[1][0]: e.bn_stats(st[:, 6:12], vp),
               (vps[1][1],), (kst,))
            op("vector", lambda e, st=st, ag=ag: e.bn_aggr(ag[:], st[:]), (kst,), (kag,))
            op("scalar", lambda e, ag=ag, lnv=lnv: e.activation(lnv[:], ag[:, 1:2],
               AF.Ln, bias=scal[:, 0:1]), (kag, "scal"), (klnv,))
            op("scalar", lambda e, lnv=lnv, rs=rs: e.activation(rs[:], lnv[:], AF.Exp, scale=-0.5),
               (klnv,), (krs,))
            op("vector", lambda e, ag=ag, rs=rs, lnv=lnv: e.scalar_tensor_tensor(
                lnv[:], ag[:, 0:1], -1.0, rs[:], ALU.mult, ALU.mult),
                (kag, krs), (klnv,))
            for eh in range(2):
                rkeys = (vps[eh][1], krs, klnv) + (("pO",) if vps[eh][2] else ())
                op("scalar", lambda e, eh=eh, vp=vps[eh][0], rs=rs, lnv=lnv: e.activation(
                    fftmp[:, eh * 512:(eh + 1) * 512], vp, AF.Identity,
                    bias=lnv[:], scale=rs[:]),
                    rkeys, ("fftmp", "WSTG"))
            op("vector", lambda e, irel=irel: e.tensor_add(
                x2sb[:, irel * D:(irel + 1) * D], x2sb[:, irel * D:(irel + 1) * D], fftmp[:]),
                ("fftmp", "fftmp0", "fftmp1", "x2:%d" % irel), ("x2:%d" % irel, "WV"))
            op("scalar", lambda e, i=i, irel=irel: e.dma_start(
                dy[i * 128:(i + 1) * 128, :], x2sb[:, irel * D:(irel + 1) * D]),
                ("x2:%d" % irel,), ("Y",), dma=True)

    n = len(OPS)
    last_writer = {}
    readers_since = {}
    deps = [set() for _ in range(n)]
    for i, o in enumerate(OPS):
        for r in o["r"]:
            if r in last_writer:
                deps[i].add(last_writer[r])
        for w in o["w"]:
            if w in last_writer:
                deps[i].add(last_writer[w])
            for rd in readers_since.get(w, ()):
                deps[i].add(rd)
        for r in o["r"]:
            readers_since.setdefault(r, []).append(i)
        for w in o["w"]:
            last_writer[w] = i
            readers_since[w] = []
    for i, o in enumerate(OPS):
        deps[i].discard(i)
        drop = set()
        for j in deps[i]:
            if OPS[j]["e"] == o["e"]:
                if o["e"] == "tensor":
                    drop.add(j)
                elif not SAME_ENGINE_SYNC:
                    drop.add(j)
        deps[i] -= drop

    need_inc = set()
    for i in range(n):
        for j in deps[i]:
            need_inc.add(j)
    slot_tot = {}
    ordinal = [0] * n
    for i, o in enumerate(OPS):
        if o.get("dma"):
            slot_tot[o["slot"]] = slot_tot.get(o["slot"], 0) + 16
            ordinal[i] = slot_tot[o["slot"]]
            need_inc.add(i)
    cum = {}
    inc_after = [None] * n
    for i, o in enumerate(OPS):
        if not o.get("dma") and i in need_inc:
            cum[o["e"]] = cum.get(o["e"], 0) + 1
        inc_after[i] = cum.get(o["e"], 0)

    sem_names = ["sync", "tensor", "scalar", "vector", "gpsimd"]
    waits = [[] for _ in range(n)]
    for i in range(n):
        best = {}
        for j in deps[i]:
            if OPS[j].get("dma"):
                key = ("D", OPS[j]["slot"])
                v = ordinal[j]
            else:
                key = ("E", OPS[j]["e"])
                v = inc_after[j]
            if best.get(key, 0) < v:
                best[key] = v
        waits[i] = sorted(best.items())

    import contextlib
    with contextlib.ExitStack() as stack:
        sems = {("E", e): stack.enter_context(nc.semaphore("sem_" + e)) for e in sem_names}
        for sl in slot_tot:
            sems[("D", sl)] = stack.enter_context(nc.semaphore("dsem_" + sl))
        block = stack.enter_context(nc.Block())
        by_eng = {e: [i for i in range(n) if OPS[i]["e"] == e] for e in sem_names}
        final_counts = {("E", e): cum.get(e, 0) for e in sem_names}
        for sl, tot in slot_tot.items():
            final_counts[("D", sl)] = tot

        def make_body(en):
            def body(eng):
                hw = {}
                for i in by_eng[en]:
                    o = OPS[i]
                    for se, val in waits[i]:
                        if hw.get(se, 0) < val:
                            eng.wait_ge(sems[se], val)
                            hw[se] = val
                    inst = o["f"](eng)
                    if o.get("dma"):
                        inst.then_inc(sems[("D", o["slot"])], 16)
                    elif i in need_inc:
                        inst.then_inc(sems[("E", en)], 1)
                if en == "gpsimd":
                    for se, v in final_counts.items():
                        if v > 0:
                            eng.wait_ge(sems[se], v)
            return body

        for en in sem_names:
            getattr(block, en)(make_body(en))
    return nc


def kernel(**inputs):
    x = np.ascontiguousarray(inputs["L0_attn_x"], dtype=np.float32)
    weights = {
        "Wq": inputs["L0_attn_Wq"], "Wk": inputs["L0_attn_Wk"], "Wv": inputs["L0_attn_Wv"],
        "bq": inputs["L0_attn_bq"], "bk": inputs["L0_attn_bk"], "bv": inputs["L0_attn_bv"],
        "Wo": inputs["L0_attn_Wo"], "bo": inputs["L0_attn_bo"],
        "U1": inputs["L0_ffn_U1"], "V1": inputs["L0_ffn_V1"], "b1": inputs["L0_ffn_b1"],
        "U2": inputs["L0_ffn_U2"], "V2": inputs["L0_ffn_V2"], "b2": inputs["L0_ffn_b2"],
    }
    weights = {k: np.ascontiguousarray(v, dtype=np.float32) for k, v in weights.items()}
    nc = build_program()
    in_maps = [dict(weights, x=np.ascontiguousarray(x[i])) for i in range(8)]
    res = run_bass_kernel_spmd(nc, in_maps, core_ids=list(range(8)))
    return np.stack([res.results[i]["y"] for i in range(8)], axis=0)


if __name__ == "__main__":
    rng = np.random.default_rng(0)
    print("building program ...")
    nc = build_program()
    print("built ok")


# version 56
# speedup vs baseline: 1.0229x; 1.0007x over previous
import math
import numpy as np
import ml_dtypes
import concourse.bass as bass
import concourse.mybir as mybir
from concourse.bass_utils import run_bass_kernel_spmd

F32 = mybir.dt.float32
BF16 = mybir.dt.bfloat16
AF = mybir.ActivationFunctionType
ALU = mybir.AluOpType
AX = mybir.AxisListType

S = 2048
D = 1024
NH = 16
DH = 64
NP = NH // 2
NT = S // 128
NC = D // 128
NB = 4
WS = 512
RK = 256
DF = 4096
NF = DF // 128
EPS = 1e-5

SAME_ENGINE_SYNC = True


def build_program(debug=False):
    nc = bass.Bass("TRN2", target_bir_lowering=False)

    dx = nc.dram_tensor("x", [S, D], F32, kind="ExternalInput").ap()
    dWq = nc.dram_tensor("Wq", [NH, D, DH], F32, kind="ExternalInput").ap()
    dWk = nc.dram_tensor("Wk", [NH, D, DH], F32, kind="ExternalInput").ap()
    dWv = nc.dram_tensor("Wv", [NH, D, DH], F32, kind="ExternalInput").ap()
    dbq = nc.dram_tensor("bq", [NH, DH], F32, kind="ExternalInput").ap()
    dbk = nc.dram_tensor("bk", [NH, DH], F32, kind="ExternalInput").ap()
    dbv = nc.dram_tensor("bv", [NH, DH], F32, kind="ExternalInput").ap()
    dWo = nc.dram_tensor("Wo", [D, D], F32, kind="ExternalInput").ap()
    dbo = nc.dram_tensor("bo", [D], F32, kind="ExternalInput").ap()
    dU1 = nc.dram_tensor("U1", [D, RK], F32, kind="ExternalInput").ap()
    dV1 = nc.dram_tensor("V1", [RK, DF], F32, kind="ExternalInput").ap()
    db1 = nc.dram_tensor("b1", [DF], F32, kind="ExternalInput").ap()
    dU2 = nc.dram_tensor("U2", [DF, RK], F32, kind="ExternalInput").ap()
    dV2 = nc.dram_tensor("V2", [RK, D], F32, kind="ExternalInput").ap()
    db2 = nc.dram_tensor("b2", [D], F32, kind="ExternalInput").ap()
    dy = nc.dram_tensor("y", [S, D], F32, kind="ExternalOutput").ap()
    dV1i = nc.dram_tensor("V1i", [NF, 2, 128, 128], BF16, kind="Internal").ap()
    dU2i = nc.dram_tensor("U2i", [NF, 128, RK], BF16, kind="Internal").ap()

    dbg = {}
    def dbg_out(name, shape):
        if debug:
            dbg[name] = nc.dram_tensor("dbg_" + name, shape, F32, kind="ExternalOutput").ap()

    bf = ml_dtypes.bfloat16
    tri = np.triu(np.ones((128, 128))).astype(bf)
    c_ident = nc.inline_tensor(np.eye(128).astype(bf), "c_ident").ap()
    c_tri2 = nc.inline_tensor(np.concatenate([tri, tri], axis=1), "c_tri2").ap()
    c_or_bf = nc.inline_tensor(np.ones((1, 128), dtype=bf), "c_or_bf").ap()
    c_or_f32 = nc.inline_tensor(np.ones((1, 128), dtype=np.float32), "c_or_f32").ap()
    c_oc_bf = nc.inline_tensor(np.ones((128, 1), dtype=bf), "c_oc_bf").ap()
    c_oc_f32 = nc.inline_tensor(np.ones((128, 1), dtype=np.float32), "c_oc_f32").ap()
    _sc = np.zeros((128, 2), dtype=np.float32)
    _sc[:, 0] = EPS
    _sc[:, 1] = -math.log(8.0)
    c_scal = nc.inline_tensor(_sc, "c_scal").ap()

    A = nc.alloc_sbuf_tensor
    xnt = A("xnt", [128, NC * S], BF16)
    xnt_addr = nc.lookup_mloc(xnt).addr
    wobf = nc.alloc_sbuf_tensor_at("wobf", [128, 8192], BF16, offset=xnt_addr + 8192 * 2)
    QT = A("QT", [128, NP * S], BF16)
    KT = A("KT", [128, NP * S], BF16)
    Vst = A("Vst", [128, NT * 1056 + 64], BF16)
    x2sb = A("x2sb", [128, 4 * D], F32)
    x2_addr = nc.lookup_mloc(x2sb).addr
    wv_all = nc.alloc_sbuf_tensor_at("wv_all", [128, 8192], BF16, offset=x2_addr)
    OT = A("OT", [128, NC * WS], BF16)
    pt = [A("pt0", [128, 1024], BF16), A("pt1", [128, 1024], BF16), None]
    xn1tb = A("xn1tb", [128, 2 * D], BF16)
    xn1t = [xn1tb[:, 0:D], xn1tb[:, D:2 * D]]
    wstage2 = nc.alloc_sbuf_tensor_at("wstage2", [128, 1024], F32, offset=nc.lookup_mloc(xn1tb).addr)

    fftmp = A("fftmp", [128, D], F32)
    ff_addr = nc.lookup_mloc(fftmp).addr
    wstage = nc.alloc_sbuf_tensor_at("wstage", [128, 1024], F32, offset=ff_addr)
    xld = [A("xld0", [128, D], F32), A("xld1", [128, D], F32)]
    qt_addr = nc.lookup_mloc(QT).addr
    xld.append(nc.alloc_sbuf_tensor_at("xld2", [128, D], F32, offset=qt_addr))
    xld.append(nc.alloc_sbuf_tensor_at("xld3", [128, D], F32, offset=qt_addr + 4096))

    zbcP = A("zbcP", [128, 512], F32)
    lz = nc.alloc_sbuf_tensor_at("lz", [1, 1024], F32, offset=nc.lookup_mloc(xnt).addr + 8192)
    wbf = [A("wbf0", [128, 1024], BF16), A("wbf1", [128, 1024], BF16)]
    wbf0_addr = nc.lookup_mloc(wbf[0]).addr
    wbf1_addr = nc.lookup_mloc(wbf[1]).addr
    PTb = nc.alloc_sbuf_tensor_at("PTb", [128, 1024], BF16, offset=wbf0_addr)
    QFT = nc.alloc_sbuf_tensor_at("QFT", [128, 1024], BF16, offset=wbf1_addr)
    Hbf = [A("Hbf0", [128, 512], BF16), A("Hbf1", [128, 512], BF16), None]
    sq = nc.alloc_sbuf_tensor_at("sq", [128, 1024], BF16, offset=nc.lookup_mloc(OT).addr)
    sqB = nc.alloc_sbuf_tensor_at("sqB", [128, 1024], BF16, offset=nc.lookup_mloc(OT).addr + 2048)
    pt[2] = nc.alloc_sbuf_tensor_at("pt2", [128, 1024], BF16, offset=nc.lookup_mloc(xnt).addr + 12288)
    wst2 = nc.alloc_sbuf_tensor_at("wst2", [128, 512], F32, offset=nc.lookup_mloc(OT).addr + 4096)
    trfb = nc.alloc_sbuf_tensor_at("trfb", [128, 512], BF16, offset=nc.lookup_mloc(OT).addr + 6144)
    Hbf[2] = nc.alloc_sbuf_tensor_at("Hbf2", [128, 512], BF16, offset=nc.lookup_mloc(OT).addr + 7168)
    wst2b = A("wst2b", [128, 512], F32)
    trfbb = A("trfbb", [128, 512], BF16)
    brstage = nc.alloc_sbuf_tensor_at("brstage", [1, 1024], F32, offset=ff_addr)
    U1bf = A("U1bf", [128, 2048], BF16)
    V2bf = A("V2bf", [128, 2048], BF16)
    V1t = [A("V1t%d" % q, [128, 256], BF16) for q in range(5)]

    U2t = [A("U2t%d" % q, [128, 256], BF16) for q in range(5)]

    stb = [A("st%d" % q, [128, 12], F32) for q in range(8)]
    agb = [A("ag%d" % q, [128, 2], F32) for q in range(8)]
    lnvb = [A("lnv%d" % q, [128, 1], F32) for q in range(8)]
    rsb = [A("rs%d" % q, [128, 1], F32) for q in range(8)]
    qs1 = A("qs1", [128, 2], F32)
    st32 = A("st32", [128, 32], F32)
    t2b = A("t2b", [128, 32], F32)
    n8b = A("n8b", [128, 32], F32)
    btqk = A("btqk", [1, 16], F32)
    ltq = A("ltq", [1, 16], F32)
    lsq = A("lsq", [1, 8], F32)
    Brow = A("Brow", [1, 8], F32)
    nBrow = A("nBrow", [1, 8], F32)
    negBb = A("negBb", [128, 8], F32)
    bqs = A("bqs", [128, 8], F32)
    bks = A("bks", [128, 8], F32)
    b1s = A("b1s", [128, 32], F32)
    borow = A("borow", [1, 1024], BF16)
    b2row = A("b2row", [1, 1024], BF16)
    bvrow = A("bvrow", [1, 1024], BF16)

    ident = A("ident", [128, 128], BF16)
    tri2 = A("tri2", [128, 256], BF16)
    or_bf = A("or_bf", [1, 128], BF16)
    or_f32 = A("or_f32", [1, 128], F32)
    oc_bf = A("oc_bf", [128, 1], BF16)
    oc_f32 = A("oc_f32", [128, 1], F32)
    scal = A("scal", [128, 2], F32)

    P = nc.alloc_psum_tensor
    pS = P("pS", [128, 1024], F32)
    pO = P("pO", [128, 1024], F32)
    pG12 = P("pG12", [128, 1024], F32)
    pG1 = pG12[:, 0:512]
    pG2 = pG12[:, 512:1024]
    pG3 = P("pG3", [128, 512], F32)
    pTP = P("pTP", [128, 1024], BF16)

    OPS = []

    def op(engine, fn, reads=(), writes=(), dma=False, slot=None):
        OPS.append({"e": engine, "f": fn, "r": tuple(reads), "w": tuple(writes),
                    "inc": 16 if dma else 1, "dma": dma,
                    "slot": (slot or (writes[0] if writes else "misc")) if dma else None})

    def warm():
        op("tensor", lambda e: e.ldweights(ident[0:1, 0:1]), ("ident",), ())

    op("gpsimd", lambda e: e.dma_start(ident[:], c_ident), (), ("ident",), dma=True)
    op("gpsimd", lambda e: e.dma_start(tri2[:], c_tri2), (), ("tri2",), dma=True)
    op("gpsimd", lambda e: e.dma_start(or_bf[:], c_or_bf), (), ("or_bf",), dma=True)
    op("gpsimd", lambda e: e.dma_start(or_f32[:], c_or_f32), (), ("or_f32",), dma=True)
    op("gpsimd", lambda e: e.dma_start(oc_bf[:], c_oc_bf), (), ("oc_bf",), dma=True)
    op("gpsimd", lambda e: e.dma_start(oc_f32[:], c_oc_f32), (), ("oc_f32",), dma=True)
    op("gpsimd", lambda e: e.dma_start(scal[:], c_scal), (), ("scal",), dma=True)
    for nm, src, dst in (("bqs", dbq, bqs), ("bks", dbk, bks)):
        op("gpsimd", lambda e, s=src, d=dst: e.dma_start(
            d[:], s.rearrange("(p two) d -> (two d) p", two=2),
            allow_slow_non_contiguous=True), (), (nm,), dma=True)
    op("gpsimd", lambda e: e.dma_start(
        b1s[:], db1.rearrange("(c P) -> P c", P=128),
        allow_slow_non_contiguous=True), (), ("b1s",), dma=True)
    for nm, src, dst in (("borow", dbo, borow), ("b2row", db2, b2row)):
        op("gpsimd", lambda e, s=src: e.dma_start(brstage[:], s.rearrange("(a d) -> a d", a=1)),
           ("WSTG",), ("brstage", "WSTG"), dma=True)
        op("vector", lambda e, d=dst: e.tensor_copy(d[:], brstage[:]), ("brstage",), (nm,))
    op("gpsimd", lambda e: e.dma_start(brstage[:], dbv.rearrange("(a h) d -> a (h d)", a=1)),
       ("WSTG",), ("brstage", "WSTG"), dma=True)
    op("vector", lambda e: e.tensor_copy(bvrow[:], brstage[:]), ("brstage",), ("bvrow",))
    op("vector", lambda e: e.memset(Vst[:], 0.0), (), ("VST",))
    op("vector", lambda e: e.memset(
        Vst[:, 0:NT * 1056].rearrange("p (t g c) -> p t g c", t=NT, g=16)[:, :, :, 64:65], 1.0),
        ("VST",), ("VST",))

    def emit_ln_stats(src_tiles, bi, extra_reads):
        st, ag, lnv, rs = stb[bi], agb[bi], lnvb[bi], rsb[bi]
        kst, kag, klnv, krs = "st%d" % bi, "ag%d" % bi, "lnv%d" % bi, "rs%d" % bi
        op("vector", lambda e: e.bn_stats(st[:, 0:6], src_tiles[0]), extra_reads, (kst,))
        op("vector", lambda e: e.bn_stats(st[:, 6:12], src_tiles[1]), extra_reads, (kst,))
        op("vector", lambda e: e.bn_aggr(ag[:], st[:]), (kst,), (kag,))
        op("scalar", lambda e: e.activation(lnv[:], ag[:, 1:2], AF.Ln, bias=scal[:, 0:1]),
           (kag, "scal"), (klnv,))
        op("scalar", lambda e: e.activation(rs[:], lnv[:], AF.Exp, scale=-0.5), (klnv,), (krs,))
        warm()
        return ag, rs, kag, krs

    def emit_ln1_tp(i):
        nb = i % 2
        kn = "xn1t%d" % nb
        for c in range(NC):
            op("tensor", lambda e, nb=nb, c=c: e.transpose(
                pTP[:, c * 128:(c + 1) * 128], xn1t[nb][:, c * 128:(c + 1) * 128], ident[:]),
                (kn, "ident"), ("pTP",))
        if i % 2 == 1:
            op("scalar", lambda e, i=i: e.activation(
                xnt[:].rearrange("p (c s) -> p c s", c=NC)[:, :, i * 128:(i + 1) * 128],
                pTP[:].rearrange("p (c s) -> p c s", c=NC), AF.Copy),
                ("pTP",), ("XNT", "xnti%d" % i))
        else:
            op("vector", lambda e, i=i: e.tensor_copy(
                xnt[:].rearrange("p (c s) -> p c s", c=NC)[:, :, i * 128:(i + 1) * 128],
                pTP[:].rearrange("p (c s) -> p c s", c=NC)),
                ("pTP",), ("XNT", "xnti%d" % i))

    for i in range(NT):
        xb = i % 4
        kx = "xld%d" % xb
        exq = ("QTm",) if xb >= 2 else ()
        xeng = "gpsimd" if i % 2 == 0 else "scalar"
        op(xeng, lambda e, i=i, xb=xb: e.dma_start(xld[xb][:], dx[i * 128:(i + 1) * 128, :]),
           (), (kx,), dma=True)
        bi = i % 4
        ag, rs, kag, krs = emit_ln_stats(
            [xld[xb][:, 0:512], xld[xb][:, 512:1024]], bi, (kx,) + exq)
        nb = i % 2
        kn = "xn1t%d" % nb
        if i % 2 == 1:
            lnv, klnv = lnvb[bi], "lnv%d" % bi
            op("vector", lambda e, ag=ag, rs=rs, lnv=lnv: e.scalar_tensor_tensor(
                lnv[:], ag[:, 0:1], -1.0, rs[:], ALU.mult, ALU.mult),
                (kag, krs), (klnv,))
            op("scalar", lambda e, xb=xb, nb=nb, rs=rs, lnv=lnv: e.activation(
                xn1t[nb], xld[xb][:], AF.Identity, bias=lnv[:], scale=rs[:]),
                (kx, krs, klnv) + exq, (kn,))
        else:
            op("vector", lambda e, xb=xb, nb=nb, ag=ag, rs=rs: e.tensor_scalar(
                xn1t[nb], xld[xb][:], ag[:, 0:1], rs[:], ALU.subtract, ALU.mult),
                (kx, kag, krs) + exq, (kn,))
        if i >= 1:
            emit_ln1_tp(i - 1)
    emit_ln1_tp(NT - 1)

    stage_cnt = [0]

    def stage_into(emitters, dstfn, kd, extra_w=()):
        par = stage_cnt[0] % 2
        stage_cnt[0] += 1
        ws = wstage if par == 0 else wstage2
        kws = "WSTG.%d" % par
        ex = ("WSTG",) if par == 0 else ("xn1t0", "xn1t1")
        for fn in emitters:
            op("sync", lambda e, fn=fn, ws=ws: fn(e, ws), (kws,), (kws,) + ex, dma=True)
        op("vector", lambda e, ws=ws: dstfn(e, ws), (kws,), (kd,) + ex + tuple(extra_w))

    def stage_w_pair(dW, p, dstbuf):
        kd = "wbf%d" % dstbuf
        stage_into(
            [lambda e, ws, dW=dW, p=p: e.dma_start(
                ws[:].rearrange("p (c d) -> p c d", c=NC)[:, :, 0:64],
                dW[2 * p].rearrange("(c P) d -> P c d", P=128)),
             lambda e, ws, dW=dW, p=p: e.dma_start(
                ws[:].rearrange("p (c d) -> p c d", c=NC)[:, :, 64:128],
                dW[2 * p + 1].rearrange("(c P) d -> P c d", P=128))],
            lambda e, ws, b=dstbuf: e.tensor_copy(wbf[b][:], ws[:]), kd)
        return kd

    def emit_transform_loads(f):
        par = f % 2
        w2 = wst2 if par == 0 else wst2b
        kw2v = "WST2V.%d" % par
        kw2u = "WST2U.%d" % par
        ex = ("OTp4", "OTp5") if par == 0 else ()
        op("gpsimd", lambda e, f=f, w2=w2: e.dma_start(
            w2[:, 0:256].rearrange("p (rc c) -> p rc c", rc=2),
            dV1[:, f * 128:(f + 1) * 128].rearrange("(rc P) c -> P rc c", P=128)),
            (kw2v,), (kw2v,) + ex, dma=True, slot="W2V%d" % par)
        op("gpsimd", lambda e, f=f, w2=w2: e.dma_start(w2[:, 256:512], dU2[f * 128:(f + 1) * 128, :]),
           (kw2u,), (kw2u,) + ex, dma=True, slot="W2U%d" % par)

    def emit_transform_rest(f):
        par = f % 2
        w2, tb = (wst2, trfb) if par == 0 else (wst2b, trfbb)
        kw2v = "WST2V.%d" % par
        kw2u = "WST2U.%d" % par
        ktbv = "TRFBV.%d" % par
        ktbu = "TRFBU.%d" % par
        ex = ("OTp4", "OTp5", "OTp6") if par == 0 else ()
        op("gpsimd", lambda e, w2=w2, tb=tb: e.tensor_copy(tb[:, 0:256], w2[:, 0:256]),
           (kw2v,), (ktbv, kw2v) + ex)
        op("gpsimd", lambda e, f=f, tb=tb: e.dma_start(
            dV1i[f].rearrange("rc p c -> p rc c"),
            tb[:, 0:256].rearrange("p (rc c) -> p rc c", rc=2)),
            (ktbv,) + ex, ("V1I",), dma=True, slot="V1I")
        op("gpsimd", lambda e, w2=w2, tb=tb: e.tensor_copy(tb[:, 256:512], w2[:, 256:512]),
           (kw2u,), (ktbu, kw2u) + ex)
        op("gpsimd", lambda e, f=f, tb=tb: e.dma_start(dU2i[f], tb[:, 256:512]),
           (ktbu,) + ex, ("U2I",), dma=True, slot="U2I")

    for p in range(NP):
        for f in range(4 * p, 4 * p + 4):
            emit_transform_loads(f)
            if f >= 1:
                emit_transform_rest(f - 1)
        for proj, dW, dst, kdst, bias in (("q", dWq, QT, "QTm", bqs), ("k", dWk, KT, "KTm", bks)):
            wb = (2 * p + (0 if proj == "q" else 1)) % 2
            kw = stage_w_pair(dW, p, wb)
            kbias = "bqs" if proj == "q" else "bks"
            for b in range(NB):
                pg, kpg = (pG1, "pG1") if b % 2 == 0 else (pG2, "pG2")
                for c in range(NC):
                    op("tensor", lambda e, wb=wb, c=c, b=b, p=p, pg=pg: e.matmul(
                        pg[:], wbf[wb][:, c * 128:(c + 1) * 128],
                        xnt[:, c * S + b * WS: c * S + (b + 1) * WS],
                        start=(c == 0), stop=(c == NC - 1)),
                        (kw, "XNT"), (kpg,))
                op("scalar", lambda e, p=p, b=b, dst=dst, bias=bias, pg=pg: e.activation(
                    dst[:, p * S + b * WS: p * S + (b + 1) * WS], pg[:], AF.Identity,
                    bias=bias[:, p:p + 1]), (kpg, kbias), (kdst,))

    emit_transform_rest(NF - 1)
    for c in range(NC):
        for o in range(2):
            stage_into(
                [lambda e, ws, c=c, o=o: e.dma_start(
                    ws[:, 0:512].rearrange("p (h d) -> p h d", h=8),
                    dWv[8 * o:8 * o + 8, c * 128:(c + 1) * 128, :].rearrange("h P d -> P h d"))],
                lambda e, ws, c=c, o=o: e.tensor_copy(
                    wv_all[:, c * 1024 + o * 512: c * 1024 + (o + 1) * 512], ws[:, 0:512]), "WV")
    for i in range(NT):
        for o in range(2):
            pg, kpg = (pG2, "pG2") if (2 * i + o) % 2 == 0 else (pG3, "pG3")
            for c in range(NC):
                op("tensor", lambda e, i=i, o=o, c=c, pg=pg: e.matmul(
                    pg[:], xnt[:, c * S + i * 128: c * S + (i + 1) * 128],
                    wv_all[:, c * 1024 + o * 512: c * 1024 + (o + 1) * 512],
                    start=(c == 0), stop=False), ("XNT", "WV"), (kpg,))
            op("tensor", lambda e, o=o, pg=pg: e.matmul(
                pg[:], or_bf[:], bvrow[0:1, o * 512:(o + 1) * 512],
                start=False, stop=True), ("or_bf", "bvrow"), (kpg,))
            op("vector", lambda e, i=i, o=o, pg=pg: e.tensor_copy(
                Vst[:, i * 1056 + (4 * o) * 132: i * 1056 + (4 * o + 4) * 132]
                   .rearrange("p (pr h d) -> p pr h d", pr=4, h=2)[:, :, :, 0:64],
                pg[:].rearrange("p (pr h d) -> p pr h d", pr=4, h=2)),
                (kpg,), ("VST",))

    for p in range(NP):
        for ti, (src, kt) in enumerate(((QT, "QTm"), (KT, "KTm"))):
            for half in range(2):
                sqb, ksq = (sq, "sq") if (2 * ti + half) % 2 == 0 else (sqB, "sqB")
                kex = ("OTp0", "OTp1") if (2 * ti + half) % 2 == 0 else ("OTp2", "OTp3")
                op("vector", lambda e, src=src, p=p, half=half, sqb=sqb: e.tensor_mul(
                    sqb[:], src[:, p * S + half * 1024: p * S + (half + 1) * 1024],
                    src[:, p * S + half * 1024: p * S + (half + 1) * 1024]), (kt,), (ksq,) + kex)
                for j in range(8):
                    col = ti * 16 + half * 8 + j
                    op("tensor", lambda e, j=j, col=col, sqb=sqb: e.matmul(
                        pG3[:, col:col + 1], sqb[:, j * 128:(j + 1) * 128], oc_bf[:],
                        start=True, stop=True), (ksq, "oc_bf"), ("pG3",))
        op("vector", lambda e: e.tensor_copy(st32[:], pG3[:, 0:32]), ("pG3",), ("st32",))
        op("vector", lambda e: e.tensor_mul(t2b[:], st32[:], st32[:]), ("st32",), ("t2b",))
        op("vector", lambda e: e.tensor_mul(n8b[:], t2b[:], t2b[:]), ("t2b",), ("n8b",))
        op("vector", lambda e: e.tensor_reduce(qs1[:, 0:1], n8b[:, 0:16], AX.X, ALU.add),
           ("n8b",), ("qs1",))
        op("vector", lambda e: e.tensor_reduce(qs1[:, 1:2], n8b[:, 16:32], AX.X, ALU.add),
           ("n8b",), ("qs1",))
        op("tensor", lambda e, p=p: e.matmul(pG3[0:1, 40 + p:41 + p], qs1[:, 0:1], oc_f32[:],
                                             start=True, stop=True), ("qs1", "oc_f32"), ("pG3",))
        op("tensor", lambda e, p=p: e.matmul(pG3[0:1, 48 + p:49 + p], qs1[:, 1:2], oc_f32[:],
                                             start=True, stop=True), ("qs1", "oc_f32"), ("pG3",))
    op("vector", lambda e: e.tensor_copy(btqk[:], pG3[0:1, 40:56]), ("pG3",), ("btqk",))
    op("scalar", lambda e: e.activation(ltq[:], btqk[:], AF.Ln), ("btqk",), ("ltq",))
    op("vector", lambda e: e.tensor_add(lsq[:], ltq[0:1, 0:8], ltq[0:1, 8:16]), ("ltq",), ("lsq",))
    op("scalar", lambda e: e.activation(Brow[:], lsq[:], AF.Exp, scale=0.125,
                                        bias=scal[0:1, 1:2]), ("lsq", "scal"), ("Brow",))
    op("vector", lambda e: e.tensor_scalar_mul(nBrow[:], Brow[:], -1.0), ("Brow",), ("nBrow",))
    for p in range(NP):
        op("tensor", lambda e, p=p: e.matmul(pG3[:, 64 + p:65 + p], or_f32[:],
                                             nBrow[0:1, p:p + 1], start=True, stop=True),
           ("nBrow", "or_f32"), ("pG3",))
    op("vector", lambda e: e.tensor_copy(negBb[:], pG3[:, 64:72]), ("pG3",), ("negBb",))

    for c in range(NC):
        stage_into(
            [lambda e, ws, c=c: e.dma_start(ws[:], dWo[c * 128:(c + 1) * 128, :])],
            lambda e, ws, c=c: e.tensor_copy(wobf[:, c * 1024:(c + 1) * 1024], ws[:]),
            "wobf%d" % c, extra_w=("XNT",))
    for half in range(2):
        stage_into(
            [lambda e, ws, half=half: e.dma_start(
                ws[:].rearrange("p (c r) -> p c r", c=4),
                dU1[half * 512:(half + 1) * 512, :].rearrange("(c P) r -> P c r", P=128))],
            lambda e, ws, half=half: e.tensor_copy(
                U1bf[:, half * 1024:(half + 1) * 1024], ws[:]), "U1bf")
    for rc in range(2):
        stage_into(
            [lambda e, ws, rc=rc: e.dma_start(ws[:], dV2[rc * 128:(rc + 1) * 128, :])],
            lambda e, ws, rc=rc: e.tensor_copy(V2bf[:, rc * 1024:(rc + 1) * 1024], ws[:]), "V2bf")

    gchunk = [0]
    for k in range(NB):
        ntc = 4 * (k + 1)
        for pre_i in (0, 1):
            i0 = 4 * k + pre_i
            op("scalar", lambda e, i0=i0, xb=pre_i % 2: e.dma_start(
                xld[xb][:], dx[i0 * 128:(i0 + 1) * 128, :]), (), ("xld%d" % (pre_i % 2),), dma=True)

        def make_epilogue(p):
            def emit_epi1():
                op("vector", lambda e, p=p: e.tensor_copy(
                    OT[0:64, p * 512:(p + 1) * 512], pO[0:64, 0:512]),
                    ("pO",), ("OTm", "OTp%d" % p))
                op("vector", lambda e, p=p: e.tensor_copy(
                    OT[64:128, p * 512:(p + 1) * 512], pO[0:64, 512:1024]),
                    ("pO",), ("OTm", "OTp%d" % p))
                op("scalar", lambda e: e.activation(lz[:], pO[64:65, :], AF.Ln),
                   ("pO",), ("lz", "XNT"))
                warm()

            def emit_epi2():
                op("tensor", lambda e: e.matmul(pG3[0:64, :], or_f32[0:1, 0:64], lz[0:1, 0:512],
                                                start=True, stop=True), ("lz", "or_f32"), ("pG3",))
                op("tensor", lambda e: e.matmul(pG3[64:128, :], or_f32[0:1, 0:64], lz[0:1, 512:1024],
                                                start=True, stop=True), ("lz", "or_f32"), ("pG3",))
                op("scalar", lambda e: e.activation(zbcP[:], pG3[:], AF.Exp, scale=-1.0),
                   ("pG3",), ("zbcP",))
                op("vector", lambda e, p=p: e.tensor_mul(
                    OT[0:64, p * 512:(p + 1) * 512],
                    OT[0:64, p * 512:(p + 1) * 512], zbcP[0:64, :]),
                    ("zbcP", "OTm", "OTp%d" % p), ("OTm", "OTp%d" % p))
                op("vector", lambda e, p=p: e.tensor_mul(
                    OT[64:128, p * 512:(p + 1) * 512],
                    OT[64:128, p * 512:(p + 1) * 512], zbcP[64:128, :]),
                    ("zbcP", "OTm", "OTp%d" % p), ("OTm", "OTp%d" % p))
            return emit_epi1, emit_epi2

        def emit_pv(p, info, first, last):
            j, off, width, pb = info
            kpt = "pt%d" % pb
            for h in range(2):
                op("tensor", lambda e, p=p, j=j, h=h, off=off, width=width, pb=pb,
                   st=first, sp=last: e.matmul(
                    pO[:, h * 512 + off: h * 512 + off + width],
                    Vst[:, j * 1056 + p * 132 + h * 66: j * 1056 + p * 132 + h * 66 + 128],
                    pt[pb][:, h * 512 + off: h * 512 + off + width],
                    start=st, stop=sp), (kpt, "VST"), ("pO",))

        pending_epi = None
        for p in range(NP):
            pend_pv = []
            for j in range(ntc):
                r = j - 4 * k
                off = 128 * r if r >= 0 else 0
                width = WS - off
                pb = gchunk[0] % 3
                sparity = gchunk[0] % 2
                gchunk[0] += 1
                kpt = "pt%d" % pb
                kptw = (kpt, "XNT") if pb == 2 else (kpt,)
                if sparity == 0:
                    svs = [(pS[:, off:off + width], "pSa"),
                           (pS[:, 512 + off: 512 + off + width], "pSb")]
                else:
                    svs = [(pG1[:, off:off + width], "pG1"),
                           (pG2[:, off:off + width], "pG2")]
                for h in range(2):
                    op("tensor", lambda e, p=p, j=j, h=h, off=off, width=width, k=k,
                       sv=svs[h][0]: e.matmul(
                        sv,
                        KT[h * 64:(h + 1) * 64, p * S + j * 128: p * S + (j + 1) * 128],
                        QT[h * 64:(h + 1) * 64, p * S + k * WS + off: p * S + k * WS + off + width],
                        start=True, stop=True, tile_position=(64 * h, 0)),
                        ("QTm", "KTm"), (svs[h][1],))
                if j == 1 and pending_epi is not None:
                    pending_epi[0]()
                if j == 3 and pending_epi is not None:
                    pending_epi[1]()
                    pending_epi = None
                if len(pend_pv) >= 2:
                    info = pend_pv.pop(0)
                    emit_pv(p, info, first=(info[0] == 0), last=False)
                if sparity == 0:
                    op("scalar", lambda e, p=p, pb=pb, off=off, width=width: e.activation(
                        pt[pb][:].rearrange("p (h s) -> p h s", h=2)[:, :, off:off + width],
                        pS[:].rearrange("p (h s) -> p h s", h=2)[:, :, off:off + width],
                        AF.Exp, bias=negBb[:, p:p + 1], scale=0.125),
                        ("pSa", "pSb", "negBb"), kptw)
                else:
                    op("scalar", lambda e, p=p, pb=pb, off=off, width=width: e.activation(
                        pt[pb][:].rearrange("p (h s) -> p h s", h=2)[:, :, off:off + width],
                        pG12[:].rearrange("p (h s) -> p h s", h=2)[:, :, off:off + width],
                        AF.Exp, bias=negBb[:, p:p + 1], scale=0.125),
                        ("pG1", "pG2", "negBb"), kptw)
                if r >= 0:
                    op("vector", lambda e, pb=pb, off=off: e.tensor_mul(
                        pt[pb][:].rearrange("p (h s) -> p h s", h=2)[:, :, off:off + 128],
                        pt[pb][:].rearrange("p (h s) -> p h s", h=2)[:, :, off:off + 128],
                        tri2[:].rearrange("p (h s) -> p h s", h=2)),
                        (kpt, "tri2"), kptw)
                pend_pv.append((j, off, width, pb))
                warm()
            if pending_epi is not None:
                pending_epi()
                pending_epi = None
            while pend_pv:
                info = pend_pv.pop(0)
                emit_pv(p, info, first=(info[0] == 0), last=(len(pend_pv) == 0))
            pending_epi = make_epilogue(p)
        pending_epi[0]()
        pending_epi[1]()
        pending_epi = None
        def emit_ln3_stats(irel):
            bi3 = 4 + irel
            kx2 = "x2:%d" % irel
            ag3, rs3, kag3, krs3 = emit_ln_stats(
                [x2sb[:, irel * D: irel * D + 512], x2sb[:, irel * D + 512: (irel + 1) * D]],
                bi3, (kx2,))
            kn3 = "xn1t%d" % (irel % 2)
            op("vector", lambda e, irel=irel, nb=irel % 2, ag=ag3, rs=rs3: e.tensor_scalar(
                xn1t[nb], x2sb[:, irel * D:(irel + 1) * D], ag[:, 0:1], rs[:],
                ALU.subtract, ALU.mult), (kx2, kag3, krs3), (kn3,))

        def emit_ln3_tp(irel):
            kn3 = "xn1t%d" % (irel % 2)
            for c in range(NC):
                op("tensor", lambda e, nb=irel % 2, c=c: e.transpose(
                    pTP[:, c * 128:(c + 1) * 128], xn1t[nb][:, c * 128:(c + 1) * 128], ident[:]),
                    (kn3, "ident"), ("pTP",))
            op("scalar", lambda e, irel=irel: e.activation(
                xnt[:, 0:4096].rearrange("p (c s) -> p c s", c=NC)[:, :, irel * 128:(irel + 1) * 128],
                pTP[:].rearrange("p (c s) -> p c s", c=NC), AF.Copy),
                ("pTP",), ("XNT",) + tuple("xnti%d" % q for q in range(16)))
            warm()

        for irel in range(4):
            i = 4 * k + irel
            xb = i % 2
            kx = "xld%d" % xb
            if irel % 2 == 0:
                ehbanks = ((0, pG1, "pG1"), (1, pG2, "pG2"))
            else:
                ehbanks = ((0, pS[:, 0:512], "pSa"), (1, pS[:, 512:1024], "pSb"))
            for eh, pg, kpg in ehbanks:
                for c in range(NC):
                    op("tensor", lambda e, c=c, irel=irel, eh=eh, pg=pg: e.matmul(
                        pg, OT[:, c * 512 + irel * 128: c * 512 + (irel + 1) * 128],
                        wobf[:, c * 1024 + eh * 512: c * 1024 + (eh + 1) * 512],
                        start=(c == 0), stop=False), ("OTm", "wobf%d" % c), (kpg,))
                op("tensor", lambda e, eh=eh, pg=pg: e.matmul(
                    pg, or_bf[:], borow[0:1, eh * 512:(eh + 1) * 512],
                    start=False, stop=True), ("or_bf", "borow"), (kpg,))
                warm()
            bi = i % 4
            st, ag, lnv, rs = stb[bi], agb[bi], lnvb[bi], rsb[bi]
            kst, kag, klnv, krs = "st%d" % bi, "ag%d" % bi, "lnv%d" % bi, "rs%d" % bi
            op("vector", lambda e, st=st, pg=ehbanks[0][1], kpg=ehbanks[0][2]: e.bn_stats(
                st[:, 0:6], pg), (ehbanks[0][2],), (kst,))
            op("vector", lambda e, st=st, pg=ehbanks[1][1]: e.bn_stats(
                st[:, 6:12], pg), (ehbanks[1][2],), (kst,))
            op("vector", lambda e, st=st, ag=ag: e.bn_aggr(ag[:], st[:]), (kst,), (kag,))
            op("scalar", lambda e, ag=ag, lnv=lnv: e.activation(lnv[:], ag[:, 1:2],
               AF.Ln, bias=scal[:, 0:1]), (kag, "scal"), (klnv,))
            op("scalar", lambda e, lnv=lnv, rs=rs: e.activation(rs[:], lnv[:], AF.Exp, scale=-0.5),
               (klnv,), (krs,))
            op("vector", lambda e, ag=ag, rs=rs, lnv=lnv: e.scalar_tensor_tensor(
                lnv[:], ag[:, 0:1], -1.0, rs[:], ALU.mult, ALU.mult),
                (kag, krs), (klnv,))
            for eh, pg, kpg in ehbanks:
                op("scalar", lambda e, pg=pg, eh=eh, rs=rs, lnv=lnv: e.activation(
                    fftmp[:, eh * 512:(eh + 1) * 512], pg, AF.Identity,
                    bias=lnv[:], scale=rs[:]),
                    (kpg, krs, klnv), ("fftmp%d" % eh, "WSTG"))
                op("vector", lambda e, irel=irel, eh=eh, xb=xb: e.tensor_add(
                    x2sb[:, irel * D + eh * 512: irel * D + (eh + 1) * 512],
                    xld[xb][:, eh * 512:(eh + 1) * 512],
                    fftmp[:, eh * 512:(eh + 1) * 512]),
                    ("fftmp%d" % eh, kx), ("x2:%d" % irel, "WV"))
            if irel < 2:
                i2 = 4 * k + irel + 2
                op("scalar", lambda e, i2=i2, xb=irel % 2: e.dma_start(
                    xld[xb][:], dx[i2 * 128:(i2 + 1) * 128, :]), (), ("xld%d" % (irel % 2),), dma=True)
            if irel >= 1:
                emit_ln3_stats(irel - 1)
            if irel >= 2:
                emit_ln3_tp(irel - 2)
        emit_ln3_stats(3)
        emit_ln3_tp(2)
        emit_ln3_tp(3)
        for rc in range(2):
            for c in range(NC):
                op("tensor", lambda e, rc=rc, c=c: e.matmul(
                    pG3[:], U1bf[:, c * 256 + rc * 128: c * 256 + (rc + 1) * 128],
                    xnt[:, c * 512:(c + 1) * 512],
                    start=(c == 0), stop=(c == NC - 1)), ("U1bf", "XNT"), ("pG3",))
            op("vector", lambda e, rc=rc: e.tensor_copy(PTb[:, rc * 512:(rc + 1) * 512], pG3[:]),
               ("pG3",), ("PTb", "wbf0"))
        def emit_u2(f):
            vb = f % 5
            hb = f % 3
            ku = "U2t%d" % vb
            kh = "Hbf%d" % hb if hb < 2 else "Hbf2"
            rks = (ku, kh, "OTp7") if hb == 2 else (ku, kh)
            op("tensor", lambda e, f=f, vb=vb, hb=hb: e.matmul(
                pG1, U2t[vb][:, 0:128], Hbf[hb][:],
                start=(f == 0), stop=(f == NF - 1)), rks, ("pG1",))
            op("tensor", lambda e, f=f, vb=vb, hb=hb: e.matmul(
                pG2, U2t[vb][:, 128:256], Hbf[hb][:],
                start=(f == 0), stop=(f == NF - 1)), rks, ("pG2",))

        def emit_ffn_loads(f):
            vb = f % 5
            kv, ku = "V1t%d" % vb, "U2t%d" % vb
            op("sync", lambda e, f=f, vb=vb: e.dma_start(
                V1t[vb][:].rearrange("p (rc c) -> p rc c", rc=2), dV1i[f].rearrange("rc p c -> p rc c")),
                ("V1I",), (kv,), dma=True)
            op("gpsimd", lambda e, f=f, vb=vb: e.dma_start(U2t[vb][:], dU2i[f]),
               ("U2I",), (ku,), dma=True)

        for q in range(4):
            emit_ffn_loads(q)
        for f in range(NF):
            vb = f % 5
            hb = f % 3
            kv, ku = "V1t%d" % vb, "U2t%d" % vb
            kh = "Hbf%d" % hb if hb < 2 else "Hbf2"
            hp, khp = [(pG3[:], "pG3"), (pS[:, 0:512], "pSa"), (pS[:, 512:1024], "pSb")][f % 3]
            op("tensor", lambda e, vb=vb, hp=hp: e.matmul(hp, V1t[vb][:, 0:128], PTb[:, 0:512],
                                                   start=True, stop=False), (kv, "PTb"), (khp,))
            op("tensor", lambda e, vb=vb, hp=hp: e.matmul(hp, V1t[vb][:, 128:256], PTb[:, 512:1024],
                                                   start=False, stop=True), (kv, "PTb"), (khp,))
            if f >= 1:
                emit_u2(f - 1)
            op("scalar", lambda e, f=f, hb=hb, hp=hp: e.activation(
                Hbf[hb][:], hp, AF.Gelu_apprx_tanh, bias=b1s[:, f:f + 1]),
                (khp, "b1s"), (kh, "OTp7") if hb == 2 else (kh,))
            if f + 4 < NF:
                emit_ffn_loads(f + 4)
        emit_u2(NF - 1)
        op("vector", lambda e: e.tensor_copy(QFT[:, 0:512], pG1), ("pG1",), ("QFT", "wbf1"))
        op("vector", lambda e: e.tensor_copy(QFT[:, 512:1024], pG2), ("pG2",), ("QFT", "wbf1"))
        v2banks = [(pG3[:], "pG3", False), (pO[:, 0:512], "pOh0", True),
                   (pO[:, 512:1024], "pOh1", True)]
        lastslot = {}
        for s in range(8):
            lastslot[v2banks[s % 3][1]] = s
        for irel in range(4):
            i = 4 * k + irel
            vps = []
            for eh in range(2):
                vp, kvp, ispo = v2banks[(2 * irel + eh) % 3]
                wk = (kvp, "pO") if ispo else (kvp,)
                op("tensor", lambda e, irel=irel, eh=eh, vp=vp: e.matmul(
                    vp, QFT[:, irel * 128:(irel + 1) * 128],
                    V2bf[:, eh * 512: (eh + 1) * 512], start=True, stop=False),
                    ("QFT", "V2bf"), wk)
                op("tensor", lambda e, irel=irel, eh=eh, vp=vp: e.matmul(
                    vp, QFT[:, 512 + irel * 128: 512 + (irel + 1) * 128],
                    V2bf[:, 1024 + eh * 512: 1024 + (eh + 1) * 512], start=False, stop=False),
                    ("QFT", "V2bf"), wk)
                op("tensor", lambda e, eh=eh, vp=vp: e.matmul(
                    vp, or_bf[:], b2row[0:1, eh * 512:(eh + 1) * 512],
                    start=False, stop=True), ("or_bf", "b2row"), wk)
                warm()
                vps.append((vp, kvp, ispo and (2 * irel + eh) == lastslot[kvp]))
            bi = 4 + irel
            st, ag, lnv, rs = stb[bi], agb[bi], lnvb[bi], rsb[bi]
            kst, kag, klnv, krs = "st%d" % bi, "ag%d" % bi, "lnv%d" % bi, "rs%d" % bi
            op("vector", lambda e, st=st, vp=vps[0][0]: e.bn_stats(st[:, 0:6], vp),
               (vps[0][1],), (kst,))
            op("vector", lambda e, st=st, vp=vps[1][0]: e.bn_stats(st[:, 6:12], vp),
               (vps[1][1],), (kst,))
            op("vector", lambda e, st=st, ag=ag: e.bn_aggr(ag[:], st[:]), (kst,), (kag,))
            op("scalar", lambda e, ag=ag, lnv=lnv: e.activation(lnv[:], ag[:, 1:2],
               AF.Ln, bias=scal[:, 0:1]), (kag, "scal"), (klnv,))
            op("scalar", lambda e, lnv=lnv, rs=rs: e.activation(rs[:], lnv[:], AF.Exp, scale=-0.5),
               (klnv,), (krs,))
            op("vector", lambda e, ag=ag, rs=rs, lnv=lnv: e.scalar_tensor_tensor(
                lnv[:], ag[:, 0:1], -1.0, rs[:], ALU.mult, ALU.mult),
                (kag, krs), (klnv,))
            for eh in range(2):
                rkeys = (vps[eh][1], krs, klnv) + (("pO",) if vps[eh][2] else ())
                op("scalar", lambda e, eh=eh, vp=vps[eh][0], rs=rs, lnv=lnv: e.activation(
                    fftmp[:, eh * 512:(eh + 1) * 512], vp, AF.Identity,
                    bias=lnv[:], scale=rs[:]),
                    rkeys, ("fftmp", "WSTG"))
            op("vector", lambda e, irel=irel: e.tensor_add(
                x2sb[:, irel * D:(irel + 1) * D], x2sb[:, irel * D:(irel + 1) * D], fftmp[:]),
                ("fftmp", "fftmp0", "fftmp1", "x2:%d" % irel), ("x2:%d" % irel, "WV"))
            op("scalar", lambda e, i=i, irel=irel: e.dma_start(
                dy[i * 128:(i + 1) * 128, :], x2sb[:, irel * D:(irel + 1) * D]),
                ("x2:%d" % irel,), ("Y",), dma=True)

    n = len(OPS)
    last_writer = {}
    readers_since = {}
    deps = [set() for _ in range(n)]
    for i, o in enumerate(OPS):
        for r in o["r"]:
            if r in last_writer:
                deps[i].add(last_writer[r])
        for w in o["w"]:
            if w in last_writer:
                deps[i].add(last_writer[w])
            for rd in readers_since.get(w, ()):
                deps[i].add(rd)
        for r in o["r"]:
            readers_since.setdefault(r, []).append(i)
        for w in o["w"]:
            last_writer[w] = i
            readers_since[w] = []
    for i, o in enumerate(OPS):
        deps[i].discard(i)
        drop = set()
        for j in deps[i]:
            if OPS[j]["e"] == o["e"]:
                if o["e"] == "tensor":
                    drop.add(j)
                elif not SAME_ENGINE_SYNC:
                    drop.add(j)
        deps[i] -= drop

    need_inc = set()
    for i in range(n):
        for j in deps[i]:
            need_inc.add(j)
    slot_tot = {}
    ordinal = [0] * n
    for i, o in enumerate(OPS):
        if o.get("dma"):
            slot_tot[o["slot"]] = slot_tot.get(o["slot"], 0) + 16
            ordinal[i] = slot_tot[o["slot"]]
            need_inc.add(i)
    cum = {}
    inc_after = [None] * n
    for i, o in enumerate(OPS):
        if not o.get("dma") and i in need_inc:
            cum[o["e"]] = cum.get(o["e"], 0) + 1
        inc_after[i] = cum.get(o["e"], 0)

    sem_names = ["sync", "tensor", "scalar", "vector", "gpsimd"]
    waits = [[] for _ in range(n)]
    for i in range(n):
        best = {}
        for j in deps[i]:
            if OPS[j].get("dma"):
                key = ("D", OPS[j]["slot"])
                v = ordinal[j]
            else:
                key = ("E", OPS[j]["e"])
                v = inc_after[j]
            if best.get(key, 0) < v:
                best[key] = v
        waits[i] = sorted(best.items())

    import contextlib
    with contextlib.ExitStack() as stack:
        sems = {("E", e): stack.enter_context(nc.semaphore("sem_" + e)) for e in sem_names}
        for sl in slot_tot:
            sems[("D", sl)] = stack.enter_context(nc.semaphore("dsem_" + sl))
        block = stack.enter_context(nc.Block())
        by_eng = {e: [i for i in range(n) if OPS[i]["e"] == e] for e in sem_names}
        final_counts = {("E", e): cum.get(e, 0) for e in sem_names}
        for sl, tot in slot_tot.items():
            final_counts[("D", sl)] = tot

        def make_body(en):
            def body(eng):
                hw = {}
                for i in by_eng[en]:
                    o = OPS[i]
                    for se, val in waits[i]:
                        if hw.get(se, 0) < val:
                            eng.wait_ge(sems[se], val)
                            hw[se] = val
                    inst = o["f"](eng)
                    if o.get("dma"):
                        inst.then_inc(sems[("D", o["slot"])], 16)
                    elif i in need_inc:
                        inst.then_inc(sems[("E", en)], 1)
                if en == "gpsimd":
                    for se, v in final_counts.items():
                        if v > 0:
                            eng.wait_ge(sems[se], v)
            return body

        for en in sem_names:
            getattr(block, en)(make_body(en))
    return nc


def kernel(**inputs):
    x = np.ascontiguousarray(inputs["L0_attn_x"], dtype=np.float32)
    weights = {
        "Wq": inputs["L0_attn_Wq"], "Wk": inputs["L0_attn_Wk"], "Wv": inputs["L0_attn_Wv"],
        "bq": inputs["L0_attn_bq"], "bk": inputs["L0_attn_bk"], "bv": inputs["L0_attn_bv"],
        "Wo": inputs["L0_attn_Wo"], "bo": inputs["L0_attn_bo"],
        "U1": inputs["L0_ffn_U1"], "V1": inputs["L0_ffn_V1"], "b1": inputs["L0_ffn_b1"],
        "U2": inputs["L0_ffn_U2"], "V2": inputs["L0_ffn_V2"], "b2": inputs["L0_ffn_b2"],
    }
    weights = {k: np.ascontiguousarray(v, dtype=np.float32) for k, v in weights.items()}
    nc = build_program()
    in_maps = [dict(weights, x=np.ascontiguousarray(x[i])) for i in range(8)]
    res = run_bass_kernel_spmd(nc, in_maps, core_ids=list(range(8)))
    return np.stack([res.results[i]["y"] for i in range(8)], axis=0)


if __name__ == "__main__":
    rng = np.random.default_rng(0)
    print("building program ...")
    nc = build_program()
    print("built ok")
